# version 03
# baseline (speedup 1.0000x reference)
import math
import numpy as np

import concourse.bass as bass
import concourse.mybir as mybir
from concourse.bass_utils import run_bass_kernel_spmd

F32 = mybir.dt.float32
F16 = mybir.dt.float16
AF = mybir.ActivationFunctionType
ALU = mybir.AluOpType
AXX = mybir.AxisListType.X

NCORES = 8
T = 1024
TT = 512
NTT = 2
D = 2048
NK = 16
DH = 1024
NB = 8
E = 8
I_DIM = 1408
NI = 11
SI = 5632
PI = math.pi
MAGIC = 12582912.0

ENGINES = ("sp", "pe", "dve", "act", "gp")


class Sem:
    def __init__(self, h):
        self.h = h
        self.n = 0


class Prog:
    TRACKED = ("dve", "act")

    def __init__(self, nc):
        self.nc = nc
        self.q = {e: [] for e in ENGINES}
        self.pw = {e: set() for e in self.TRACKED}
        self.pr = {e: set() for e in self.TRACKED}

    def wait(self, eng, sem, val=None):
        v = sem.n if val is None else val
        if v <= 0:
            return
        self.q[eng].append({"fn": (lambda e, h=sem.h, vv=v: e.wait_ge(h, vv)), "incs": None})

    def op(self, eng, fn, inc=None, inck=1, reads=(), writes=()):
        if eng in self.TRACKED:
            rd, wr = set(reads), set(writes)
            if (rd & self.pw[eng]) or (wr & (self.pw[eng] | self.pr[eng])):
                self.q[eng].append({"fn": (lambda e: e.drain()), "incs": None})
                self.pw[eng].clear()
                self.pr[eng].clear()
            self.pw[eng] |= wr
            self.pr[eng] |= rd
        ent = {"fn": fn, "incs": []}
        if inc is not None:
            ent["incs"].append((inc.h, inck))
            inc.n += inck
        self.q[eng].append(ent)

    def milestone(self, eng, sem, k=1):
        for ent in reversed(self.q[eng]):
            if ent["incs"] is not None:
                ent["incs"].append((sem.h, k))
                sem.n += k
                return
        raise RuntimeError("no op to attach milestone to")

    def dve(self, fn, reads=(), writes=(), inc=None, inck=1):
        self.op("dve", fn, inc=inc, inck=inck, reads=reads, writes=writes)


class Pool:
    def __init__(self, tensor, nslots, width, sems):
        self.t = tensor
        self.n = nslots
        self.w = width
        self.guard = [0] * nslots
        self.sem = sems

    def slot(self, i):
        j = i % self.n
        return self.t[:, j * self.w:(j + 1) * self.w]


def build_program(taps=()):
    taps = set(taps)
    nc = bass.Bass("TRN2", target_bir_lowering=False, debug=False)

    def din(name, shape):
        return nc.dram_tensor(name, list(shape), F32, kind="ExternalInput").ap()

    x_hbm = din("x", (T, D))
    W_in = din("W_in", (D, D))
    b_in = din("b_in", (1, D))
    Wg_x = din("Wg_x", (NB, 256, 128))
    bg_x = din("bg_x", (1, DH))
    Wg_a = din("Wg_a", (NB, 256, 128))
    bg_a = din("bg_a", (1, DH))
    W_out = din("W_out", (D, D))
    b_out = din("b_out", (1, D))
    a_real = din("a_real", (1, DH))
    a_imag = din("a_imag", (1, DH))
    router_w = din("router_w", (E, D))
    w_gate = din("w_gate", (E, D, I_DIM))
    w_up = din("w_up", (E, D, I_DIM))
    w_down = din("w_down", (E, I_DIM, D))
    sw_gate = din("sw_gate", (D, SI))
    sw_up = din("sw_up", (D, SI))
    sw_down = din("sw_down", (SI, D))
    sgw = din("sgw", (1, D))
    ident_in = din("ident", (128, 128))
    oh8_in = din("oh8", (1, 8))
    e8sel_in = din("e8sel", (8, 1024))

    out_h = nc.dram_tensor("out", [T, D], F32, kind="ExternalOutput").ap()
    cc_in = [nc.dram_tensor(f"cc_in{b}", [128, 4], F32, kind="Internal").ap() for b in range(NB)]
    cc_out = [nc.dram_tensor(f"cc_out{b}", [NCORES, 128, 4], F32, kind="Internal").ap() for b in range(NB)]

    import contextlib
    es = contextlib.ExitStack()

    def sb(name, shape, dt):
        return es.enter_context(nc.sbuf_tensor(name, list(shape), dt))

    SCR = sb("SCR", (128, 8 * 1024), F32)

    def scr(i, n=1024):
        return SCR[:, i * 1024:i * 1024 + n]

    gate_b = sb("gate_b", (128, T), F16)
    ident32 = sb("ident32", (128, 128), F32)
    ident16 = sb("ident16", (128, 128), F16)
    smalls = sb("smalls", (128, 64), F32)

    def sml(i, n=1):
        return smalls[:, i:i + n]

    bcol_in = sb("bcol_in", (128, NK), F32)
    bcol_out = sb("bcol_out", (128, NK), F32)
    bcol_gx = sb("bcol_gx", (128, NB), F32)
    bcol_ga = sb("bcol_ga", (128, NB), F32)
    sp_col = sb("sp_col", (128, NB), F32)
    ai_col = sb("ai_col", (128, NB), F32)
    rows_t = sb("rows_t", (128, D), F32)
    Crows = rows_t[32:40, 0:T]
    e8sel = rows_t[32:40, T:T + 1024]
    rw_s = rows_t[64:72, :]
    grow = rows_t[0:1, T:T + T]
    ones_t = sb("ones_t", (128, 128), F32)
    G_t = sb("G_t", (128, 32), F32)
    Hall = sb("Hall", (128, 16), F32)
    S_t = sb("S_t", (128, 4), F32)
    oh8b = sb("oh8b", (128, 8), F32)

    xnT = sb("xnT", (128, NK * 1024), F16)

    def xnT_k(k, tt=None):
        if tt is None:
            return xnT[:, k * 1024:(k + 1) * 1024]
        return xnT[:, k * 1024 + tt * TT:k * 1024 + tt * TT + TT]

    XH = sb("XH", (128, NK * 1024), F16)

    def XH_k(k, tt=None):
        if tt is None:
            return XH[:, k * 1024:(k + 1) * 1024]
        return XH[:, k * 1024 + tt * TT:k * 1024 + tt * TT + TT]

    ACC_k = XH_k
    HREG = sb("HREG", (128, 22 * 1024), F16)

    def h_k(k, tt=None):
        if tt is None:
            return HREG[:, k * 1024:(k + 1) * 1024]
        return HREG[:, k * 1024 + tt * TT:k * 1024 + tt * TT + TT]

    sh_k = h_k

    def hp_k(par, i, tt=None):
        base = (par * NI + i) * 1024
        if tt is None:
            return HREG[:, base:base + 1024]
        return HREG[:, base + tt * TT:base + tt * TT + TT]

    WPA = sb("WPA", (128, 16 * 384), F16)
    WPB = sb("WPB", (128, 16 * 384), F16)
    WPC = sb("WPC", (128, 8 * 256), F16)
    wg16 = sb("wg16", (128, 2 * NB * 2 * 128), F16)

    def wg_tile(g, b, kk):
        off = ((g * NB + b) * 2 + kk) * 128
        return wg16[:, off:off + 128]

    rwT = sb("rwT", (128, NK * 8), F16)
    sgw_col = sb("sgw_col", (128, NK), F16)
    csb = sb("csb", (128, T), F16)
    snb = sb("snb", (128, T), F16)
    xtr = sb("xtr", (128, T), F16)
    xti = sb("xti", (128, T), F16)
    grL = sb("grL", (128, T), F16)
    giL = sb("giL", (128, T), F16)
    gpr = sb("gpr", (128, T), F16)
    gpi = sb("gpi", (128, T), F16)
    zeros_h = sb("zeros_h", (128, TT), F16)

    ps = [es.enter_context(nc.psum_tensor(f"ps{i}", [128, TT], F32)) for i in range(8)]

    s_wg = Sem(es.enter_context(nc.semaphore("s_wg")))
    s_ld = Sem(es.enter_context(nc.semaphore("s_ld")))
    s_x2 = [Sem(es.enter_context(nc.semaphore(f"s_x{i}"))) for i in range(2)]
    semA = [Sem(es.enter_context(nc.semaphore(f"sA{i}"))) for i in range(16)]
    semB = [Sem(es.enter_context(nc.semaphore(f"sB{i}"))) for i in range(16)]
    semC = [Sem(es.enter_context(nc.semaphore(f"sC{i}"))) for i in range(8)]
    s_pe = Sem(es.enter_context(nc.semaphore("s_pe")))
    s_dv = Sem(es.enter_context(nc.semaphore("s_dv")))
    s_ac = Sem(es.enter_context(nc.semaphore("s_ac")))
    s_cc = Sem(es.enter_context(nc.semaphore("s_cc")))
    s_st = Sem(es.enter_context(nc.semaphore("s_st")))
    s_ex = Sem(es.enter_context(nc.semaphore("s_ex")))

    block = es.enter_context(nc.Block())
    p = Prog(nc)

    def tap(name, ap, shape, dt=None):
        if name not in taps:
            return
        th = nc.dram_tensor("tap_" + name, list(shape), dt or ap.dtype, kind="ExternalOutput").ap()
        for sem in (s_pe, s_dv, s_ac):
            p.wait("sp", sem)
        p.op("sp", lambda e, o=th, i=ap: e.dma_start(out=o, in_=i), inc=s_st, inck=16)
        for eng in ("pe", "dve", "act", "gp"):
            p.wait(eng, s_st)

    def sdma(dst, src, sem=s_ld):
        p.op("sp", lambda e, o=dst, i=src: e.dma_start(out=o, in_=i), inc=sem, inck=16)

    def wdma(dst, src, sem=None):
        sem = sem if sem is not None else s_wg
        p.op("gp", lambda e, o=dst, i=src: e.dma_start(out=o, in_=i), inc=sem, inck=16)

    def pool_load(pool, i, src, width=None):
        j = i % pool.n
        if pool.guard[j]:
            p.wait("gp", s_pe, pool.guard[j])
        w = width if width is not None else pool.w
        sl = pool.slot(i)[:, 0:w]
        sem = pool.sem[j]
        p.op("gp", lambda e, o=sl, i_=src: e.dma_start(out=o, in_=i_), inc=sem, inck=16)
        return sem, sem.n

    sdma(ident32[:], ident_in)
    sdma(e8sel[:], e8sel_in)
    wdma(ident16[:], ident_in)
    for g, W in ((0, Wg_x), (1, Wg_a)):
        for b in range(NB):
            for kk in range(2):
                wdma(wg_tile(g, b, kk), W[b, kk * 128:(kk + 1) * 128, :])
    wg_wd = s_wg.n

    p.dve(lambda e: e.memset(ones_t[:], 1.0), writes=("ones_t",))
    p.dve(lambda e: e.memset(zeros_h[:], 0.0), writes=("zeros_h",))

    vecs = [
        ("bcol_in", b_in, NK, bcol_in, None),
        ("bcol_out", b_out, NK, bcol_out, None),
        ("bcol_gx", bg_x, NB, bcol_gx, None),
        ("bcol_ga", bg_a, NB, bcol_ga, None),
        ("sp_col", a_real, NB, sp_col, "softplus"),
        ("ai_col", a_imag, NB, ai_col, None),
        ("sgw_col", sgw, NK, sgw_col, None),
    ]
    p.wait("pe", s_ld)
    pe_vec_done = []
    for vi, (vnm, src, nchunk, dstcol, post) in enumerate(vecs):
        rb = 64 * (vi % 2)
        row = rows_t[rb:rb + 1, :]
        if vi >= 2:
            p.wait("sp", s_pe, pe_vec_done[vi - 2])
        sdma(row[:, 0:128 * nchunk], src[:, 0:128 * nchunk], sem=s_x2[vi % 2])
        p.wait("pe", s_x2[vi % 2])
        if vi >= 2:
            p.wait("pe", s_ac)
        bank = ps[vi % 2]
        for c in range(nchunk):
            p.op("pe", lambda e, i=row[:, c * 128:(c + 1) * 128], o=bank[:, c:c + 1], b_=rb:
                 e.transpose(out=o, in_=i, identity=ident32[b_:b_ + 1, b_:b_ + 1]))
        p.milestone("pe", s_pe)
        pe_vec_done.append(s_pe.n)
        p.wait("act", s_pe)
        if post == "softplus":
            pcol = G_t[:, 8:16]
            mcol = G_t[:, 16:24]
            ncol = G_t[:, 24:32]
            ecol = G_t[:, 0:8]
            qcol = Hall[:, 0:8]
            lcol = Hall[:, 8:16]
            p.op("act", lambda e, o=pcol, i=bank[:, 0:nchunk]: e.copy(out=o, in_=i), inc=s_ac,
                 writes=("spp",))
            p.wait("dve", s_ac)
            p.dve(lambda e, o=mcol, i=pcol: e.tensor_scalar(out=o, in0=i, scalar1=0.0, scalar2=None, op0=ALU.max),
                  reads=("spp",), writes=("spm",))
            p.op("act", lambda e, o=ncol, i=pcol: e.activation(out=o, in_=i, func=AF.Abs), inc=s_ac,
                 reads=("spp",), writes=("spn",))
            p.op("act", lambda e, o=ecol, i=ncol: e.activation(out=o, in_=i, func=AF.Exp, scale=-1.0), inc=s_ac,
                 reads=("spn",), writes=("spe",))
            p.wait("dve", s_ac)
            p.dve(lambda e, o=qcol, i=ecol: e.tensor_scalar(out=o, in0=i, scalar1=1.0, scalar2=None, op0=ALU.add),
                  reads=("spe",), writes=("spq",), inc=s_dv)
            p.wait("act", s_dv)
            p.op("act", lambda e, o=lcol, i=qcol: e.activation(out=o, in_=i, func=AF.Ln), inc=s_ac,
                 reads=("spq",), writes=("spl",))
            p.wait("dve", s_ac)
            p.dve(lambda e, o=dstcol[:, 0:nchunk], a=mcol, bb=lcol: e.tensor_add(out=o, in0=a, in1=bb),
                  reads=("spm", "spl"), writes=("sp_col",), inc=s_dv)
            p.wait("act", s_dv)
        else:
            p.op("act", lambda e, o=dstcol[:, 0:nchunk], i=bank[:, 0:nchunk]:
                 e.copy(out=o, in_=i), inc=s_ac, writes=(vnm,))
    p.wait("sp", s_pe)
    sdma(rw_s[:], router_w)
    p.wait("pe", s_ld)
    for k in range(NK):
        bank = ps[2]
        if k > 0:
            p.wait("pe", s_ac)
        p.op("pe", lambda e, i=rw_s[0:8, k * 128:(k + 1) * 128], o=bank[:, 0:8]:
             e.transpose(out=o, in_=i, identity=ident32[64:72, 64:72]), inc=s_pe)
        p.wait("act", s_pe)
        p.op("act", lambda e, o=rwT[:, k * 8:(k + 1) * 8], i=bank[:, 0:8]:
             e.copy(out=o, in_=i), inc=s_ac, writes=("rwT",))
    row = rows_t[64:65, 0:8]
    p.wait("sp", s_pe)
    sdma(row, oh8_in)
    p.wait("pe", s_ld)
    p.wait("pe", s_ac)
    p.op("pe", lambda e, o=ps[3][:, 0:8], r=row: e.matmul(out=o, lhsT=ones_t[64:65, 0:128], rhs=r, start=True, stop=True),
         inc=s_pe)
    p.wait("dve", s_pe)
    p.dve(lambda e, o=oh8b[:], i=ps[3][:, 0:8]: e.tensor_copy(out=o, in_=i), writes=("oh8b",), inc=s_dv)
    p.wait("pe", s_dv)

    EPS = 1e-5
    ln_done_pe = []
    for tj in range(8):
        xsl = scr(4 * (tj % 2), 2048)
        xnsl = scr(4 * (tj % 2) + 2, 2048)
        xid = ("xtile", tj % 2)
        nid = ("xntile", tj % 2)
        if tj >= 2:
            p.wait("sp", s_pe, ln_done_pe[tj - 2])
        sdma(xsl, x_hbm[tj * 128:(tj + 1) * 128, :], sem=s_x2[tj % 2])
        p.wait("dve", s_x2[tj % 2])
        p.dve(lambda e, x=xsl, m=sml(0): e.reduce_sum(out=m, in_=x, axis=AXX),
              reads=(xid,), writes=("sm0",))
        p.dve(lambda e, m=sml(0): e.tensor_scalar(out=m, in0=m, scalar1=1.0 / D, scalar2=None, op0=ALU.mult),
              reads=("sm0",), writes=("sm0",))
        p.dve(lambda e, x=xsl, o=xnsl, m=sml(0):
              e.tensor_scalar(out=o, in0=x, scalar1=m, scalar2=None, op0=ALU.subtract),
              reads=(xid, "sm0"), writes=(nid,), inc=s_dv)
        p.wait("act", s_dv)
        p.op("act", lambda e, i=xnsl, o=xsl, a=sml(1):
             e.activation(out=o, in_=i, func=AF.Square, accum_out=a), inc=s_ac,
             reads=(nid,), writes=(xid, "sm1"))
        p.wait("dve", s_ac)
        p.dve(lambda e, a=sml(1), o=sml(2):
              e.tensor_scalar(out=o, in0=a, scalar1=1.0 / D, scalar2=EPS, op0=ALU.mult, op1=ALU.add),
              reads=("sm1",), writes=("sm2",))
        p.dve(lambda e, i=sml(2), o=sml(3): e.reciprocal(out=o, in_=i),
              reads=("sm2",), writes=("sm3",), inc=s_dv)
        p.wait("act", s_dv)
        p.op("act", lambda e, i=sml(3), o=sml(2): e.activation(out=o, in_=i, func=AF.Sqrt), inc=s_ac,
             reads=("sm3",), writes=("sm2",))
        p.wait("dve", s_ac)
        p.dve(lambda e, i=xnsl, m=sml(2):
              e.tensor_scalar(out=i, in0=i, scalar1=m, scalar2=None, op0=ALU.mult),
              reads=(nid, "sm2"), writes=(nid,), inc=s_dv)
        p.wait("pe", s_dv)
        for q in range(4):
            bank = ps[q]
            for j in range(4):
                k = q * 4 + j
                p.op("pe", lambda e, i=xnsl[:, k * 128:(k + 1) * 128], o=bank[:, j * 128:(j + 1) * 128]:
                     e.transpose(out=o, in_=i, identity=ident32[:]))
            p.milestone("pe", s_pe)
            p.wait("act", s_pe)
            dst = xnT[:, :].rearrange("p (k t) -> p k t", k=NK)[:, q * 4:(q + 1) * 4, tj * 128:(tj + 1) * 128]
            srcv = bank[:, 0:512].rearrange("p (k t) -> p k t", k=4)
            p.op("act", lambda e, o=dst, i=srcv: e.copy(out=o, in_=i), inc=s_ac, writes=("xnT",))
        ln_done_pe.append(s_pe.n)
        p.wait("pe", s_ac)
    tap("xnT", xnT[:], (128, NK * 1024), F16)

    poolA = Pool(WPA, 16, 384, semA)
    poolB = Pool(WPB, 16, 384, semB)
    poolC = Pool(WPC, 8, 256, semC)

    for g in range(8):
        banks = [ps[(g % 2) * 4 + i] for i in range(4)]
        lds = [pool_load(poolA, k, W_in[k * 128:(k + 1) * 128, g * 256:(g + 1) * 256], 256)
               for k in range(NK)]
        if g >= 2:
            p.wait("pe", s_ac)
        for k in range(NK):
            sl = poolA.slot(k)
            p.wait("pe", lds[k][0], lds[k][1])
            for j in range(2):
                for tt in range(NTT):
                    p.op("pe", lambda e, o=banks[j * 2 + tt][:], w=sl[:, j * 128:(j + 1) * 128],
                         r=xnT_k(k, tt), st=(k == 0), fin=(k == NK - 1):
                         e.matmul(out=o, lhsT=w, rhs=r, start=st, stop=fin))
        p.milestone("pe", s_pe)
        for k in range(NK):
            poolA.guard[k] = s_pe.n
        p.wait("act", s_pe)
        for j in range(2):
            oc = g * 2 + j
            for tt in range(NTT):
                p.op("act", lambda e, o=XH_k(oc, tt), i=banks[j * 2 + tt][:], bb=bcol_in[:, oc:oc + 1]:
                     e.activation(out=o, in_=i, func=AF.Identity, bias=bb), inc=s_ac,
                     reads=("bcol_in",), writes=("xh",))
    xh_done_ac = s_ac.n
    tap("x_h", XH[:], (128, NK * 1024), F16)

    p.wait("pe", s_ac, xh_done_ac)
    for tj in range(8):
        bank = ps[tj % 2]
        if tj >= 2:
            p.wait("pe", s_dv)
        for k in range(NK):
            p.op("pe", lambda e, o=bank[:, 0:8],
                 w=xnT[:, k * 1024 + tj * 128:k * 1024 + (tj + 1) * 128],
                 r=rwT[:, k * 8:(k + 1) * 8], st=(k == 0), fin=(k == NK - 1):
                 e.matmul(out=o, lhsT=w, rhs=r, start=st, stop=fin))
        p.milestone("pe", s_pe)
        p.wait("dve", s_pe)
        Lt = scr(6, 8)
        V8 = scr(6, 1024)[:, 16:24]
        C1 = scr(6, 1024)[:, 32:40]
        C2 = scr(6, 1024)[:, 40:48]
        p.dve(lambda e, o=Lt, i=bank[:, 0:8]: e.tensor_copy(out=o, in_=i), writes=("rlog",))
        p.dve(lambda e, o=V8, i=Lt: e.max(out=o, in_=i), reads=("rlog",), writes=("rmax",))
        p.dve(lambda e, o=sml(8), a=V8[:, 0:1], bb=V8[:, 1:2]: e.tensor_sub(out=o, in0=a, in1=bb),
              reads=("rmax",), writes=("sm8",), inc=s_dv)
        p.wait("act", s_dv)
        p.op("act", lambda e, o=sml(9), i=sml(8): e.activation(out=o, in_=i, func=AF.Sigmoid), inc=s_ac,
             reads=("sm8",), writes=("sm9",))
        p.wait("dve", s_ac)
        p.dve(lambda e, o=sml(10), i=sml(9):
              e.tensor_scalar(out=o, in0=i, scalar1=-1.0, scalar2=1.0, op0=ALU.mult, op1=ALU.add),
              reads=("sm9",), writes=("sm10",))
        p.dve(lambda e, o=C1, i=Lt, v=V8[:, 0:1], w=sml(9):
              e.tensor_scalar(out=o, in0=i, scalar1=v, scalar2=w, op0=ALU.is_equal, op1=ALU.mult),
              reads=("rlog", "rmax", "sm9"), writes=("rc1",))
        p.dve(lambda e, o=C2, i=Lt, v=V8[:, 1:2], w=sml(10):
              e.tensor_scalar(out=o, in0=i, scalar1=v, scalar2=w, op0=ALU.is_equal, op1=ALU.mult),
              reads=("rlog", "rmax", "sm10"), writes=("rc2",))
        p.dve(lambda e, o=C1, a=C1, bb=C2: e.tensor_add(out=o, in0=a, in1=bb),
              reads=("rc1", "rc2"), writes=("rc1",), inc=s_dv)
        p.wait("pe", s_dv)
        bank2 = ps[2 + tj % 2]
        p.op("pe", lambda e, o=bank2[0:8, 0:128], i=C1:
             e.transpose(out=o, in_=i, identity=ident32[:]), inc=s_pe)
        p.wait("dve", s_pe)
        p.dve(lambda e, o=Crows[0:8, tj * 128:(tj + 1) * 128], i=bank2[0:8, 0:128]:
              e.tensor_copy(out=o, in_=i), writes=("crows",), inc=s_dv)
    tap("Crows", Crows[:], (8, T))

    for tt in range(NTT):
        bank = ps[4]
        if tt > 0:
            p.wait("pe", s_dv)
        for k in range(NK):
            p.op("pe", lambda e, o=bank[0:1, :], w=sgw_col[:, k:k + 1], r=xnT_k(k, tt),
                 st=(k == 0), fin=(k == NK - 1):
                 e.matmul(out=o, lhsT=w, rhs=r, start=st, stop=fin))
        p.milestone("pe", s_pe)
        p.wait("act", s_pe)
        p.op("act", lambda e, o=grow[0:1, tt * TT:(tt + 1) * TT], i=bank[0:1, :]:
             e.activation(out=o, in_=i, func=AF.Sigmoid), inc=s_ac, writes=("grow",))
        p.wait("pe", s_ac)
        bank5 = ps[5]
        p.op("pe", lambda e, o=bank5[:], r=grow[0:1, tt * TT:(tt + 1) * TT]:
             e.matmul(out=o, lhsT=ones_t[0:1, 0:128], rhs=r, start=True, stop=True), inc=s_pe)
        p.wait("dve", s_pe)
        p.dve(lambda e, o=gate_b[:, tt * TT:(tt + 1) * TT], i=bank5[:]: e.tensor_copy(out=o, in_=i),
              writes=("gate_b",), inc=s_dv)
    tap("gate_b", gate_b[:], (128, T))

    GX, GA, LA, THm, ASQ, M1, M2 = (scr(i) for i in range(7))
    MAGt = GA
    PRE = ASQ
    TH = GX

    p.wait("pe", s_wg, wg_wd)
    gate_evac_ac = [0, 0]
    for b in range(NB):
        boff = (b % 2) * 4
        banks = [ps[boff + i] for i in range(4)]
        if b < 2:
            p.wait("pe", s_dv)
        else:
            p.wait("pe", s_ac, gate_evac_ac[b % 2])
        for g in range(2):
            for tt in range(NTT):
                for kk in range(2):
                    p.op("pe", lambda e, o=banks[g * 2 + tt][:], w=wg_tile(g, b, kk),
                         r=XH_k(2 * b + kk, tt), st=(kk == 0), fin=(kk == 1):
                         e.matmul(out=o, lhsT=w, rhs=r, start=st, stop=fin))
        p.milestone("pe", s_pe)
        p.wait("act", s_pe)
        p.wait("act", s_dv)
        for g, dstt, bc, gnm, bnm in ((0, GX, bcol_gx, "gx", "bcol_gx"), (1, GA, bcol_ga, "ga", "bcol_ga")):
            for tt in range(NTT):
                p.op("act", lambda e, o=dstt[:, tt * TT:(tt + 1) * TT], i=banks[g * 2 + tt][:],
                     bb=bc[:, b:b + 1]:
                     e.activation(out=o, in_=i, func=AF.Sigmoid, bias=bb), inc=s_ac,
                     reads=(bnm,), writes=(gnm,))
        gate_evac_ac[b % 2] = s_ac.n
        p.wait("dve", s_ac)
        p.dve(lambda e, o=LA, i=GA, s=sp_col[:, b:b + 1]:
              e.tensor_scalar(out=o, in0=i, scalar1=s, scalar2=-8.0, op0=ALU.mult, op1=ALU.mult),
              reads=("ga",), writes=("la",))
        p.dve(lambda e, o=THm, i=GA, s=ai_col[:, b:b + 1]:
              e.tensor_scalar(out=o, in0=i, scalar1=s, scalar2=None, op0=ALU.mult),
              reads=("ga",), writes=("th",), inc=s_dv)
        p.wait("act", s_dv)
        p.op("act", lambda e, o=MAGt, i=LA: e.activation(out=o, in_=i, func=AF.Exp),
             reads=("la",), writes=("ga",))
        p.op("act", lambda e, o=ASQ, i=LA: e.activation(out=o, in_=i, func=AF.Exp, scale=2.0), inc=s_ac,
             reads=("la",), writes=("asq",))
        p.dve(lambda e, o=sml(4), i=LA: e.reduce_sum(out=o, in_=i, axis=AXX),
              reads=("la",), writes=("sm4",))
        p.wait("dve", s_ac)
        p.dve(lambda e, o=M1, i=ASQ:
              e.tensor_scalar(out=o, in0=i, scalar1=-1.0, scalar2=1.0, op0=ALU.mult, op1=ALU.add),
              reads=("asq",), writes=("m1",), inc=s_dv)
        p.wait("act", s_dv)
        p.op("act", lambda e, o=M2, i=M1: e.activation(out=o, in_=i, func=AF.Sqrt), inc=s_ac,
             reads=("m1",), writes=("m2",))
        p.wait("dve", s_ac)
        p.dve(lambda e, o=PRE, a=GX, bb=M2: e.tensor_mul(out=o, in0=a, in1=bb),
              reads=("gx", "m2"), writes=("asq",))
        for tt in range(NTT):
            p.dve(lambda e, o=TH[:, tt * TT:(tt + 1) * TT], d0=THm[:, tt * TT:(tt + 1) * TT], d1=zeros_h[:]:
                  e.tensor_tensor_scan(out=o, data0=d0, data1=d1, initial=0.0, op0=ALU.add, op1=ALU.add),
                  reads=("th", "zeros_h"), writes=("gx",))
        p.dve(lambda e, o=sml(5), a=TH[:, TT - 1:TT], bb=TH[:, T - 1:T]: e.tensor_add(out=o, in0=a, in1=bb),
              reads=("gx",), writes=("sm5",))
        p.dve(lambda e, o=M2, i=TH:
              e.tensor_scalar(out=o, in0=i, scalar1=1.0 / (2 * PI), scalar2=MAGIC, op0=ALU.mult, op1=ALU.add),
              reads=("gx",), writes=("m2",))
        p.dve(lambda e, o=M2, i=M2: e.tensor_scalar(out=o, in0=i, scalar1=-MAGIC, scalar2=None, op0=ALU.add),
              reads=("m2",), writes=("m2",))
        p.dve(lambda e, o=M1, i=M2, t=TH:
              e.scalar_tensor_tensor(out=o, in0=i, scalar=-2 * PI, in1=t, op0=ALU.mult, op1=ALU.add),
              reads=("m2", "gx"), writes=("m1",))
        p.dve(lambda e, o=M1, i=M1:
              e.tensor_scalar(out=o, in0=i, scalar1=-PI, scalar2=PI, op0=ALU.max, op1=ALU.min),
              reads=("m1",), writes=("m1",), inc=s_dv)
        p.wait("act", s_dv)
        p.op("act", lambda e, o=snb[:], i=M1: e.activation(out=o, in_=i, func=AF.Sin), inc=s_ac,
             reads=("m1",), writes=("sn",))
        p.dve(lambda e, o=M2, i=M1:
              e.tensor_scalar(out=o, in0=i, scalar1=PI / 2, scalar2=-2 * PI, op0=ALU.is_gt, op1=ALU.mult),
              reads=("m1",), writes=("m2",))
        p.dve(lambda e, o=M2, i=M2, r=M1:
              e.scalar_tensor_tensor(out=o, in0=i, scalar=PI / 2, in1=r, op0=ALU.add, op1=ALU.add),
              reads=("m1", "m2"), writes=("m2",))
        p.dve(lambda e, o=M2, i=M2:
              e.tensor_scalar(out=o, in0=i, scalar1=-PI, scalar2=PI, op0=ALU.max, op1=ALU.min),
              reads=("m2",), writes=("m2",), inc=s_dv)
        p.wait("act", s_dv)
        p.op("act", lambda e, o=csb[:], i=M2: e.activation(out=o, in_=i, func=AF.Sin), inc=s_ac,
             reads=("m2",), writes=("cs",))
        p.wait("dve", s_ac)
        XR = XH_k(b)
        XI = XH_k(8 + b)
        p.dve(lambda e, o=M1, a=csb[:], bb=XR: e.tensor_mul(out=o, in0=a, in1=bb),
              reads=("cs",), writes=("m1",))
        p.dve(lambda e, o=M2, a=snb[:], bb=XI: e.tensor_mul(out=o, in0=a, in1=bb),
              reads=("sn",), writes=("m2",))
        p.dve(lambda e, o=M1, a=M1, bb=M2: e.tensor_add(out=o, in0=a, in1=bb),
              reads=("m1", "m2"), writes=("m1",))
        p.dve(lambda e, o=xtr[:], a=M1, bb=PRE: e.tensor_mul(out=o, in0=a, in1=bb),
              reads=("m1", "asq"), writes=("xtr",))
        p.dve(lambda e, o=M1, a=csb[:], bb=XI: e.tensor_mul(out=o, in0=a, in1=bb),
              reads=("cs",), writes=("m1",))
        p.dve(lambda e, o=M2, a=snb[:], bb=XR: e.tensor_mul(out=o, in0=a, in1=bb),
              reads=("sn",), writes=("m2",))
        p.dve(lambda e, o=M1, a=M1, bb=M2: e.tensor_sub(out=o, in0=a, in1=bb),
              reads=("m1", "m2"), writes=("m1",))
        p.dve(lambda e, o=xti[:], a=M1, bb=PRE: e.tensor_mul(out=o, in0=a, in1=bb),
              reads=("m1", "asq"), writes=("xti",))
        p.dve(lambda e, o=grL[:, 0:TT], d0=MAGt[:, 0:TT], d1=xtr[:, 0:TT]:
              e.tensor_tensor_scan(out=o, data0=d0, data1=d1, initial=0.0, op0=ALU.mult, op1=ALU.add),
              reads=("ga", "xtr"), writes=("grL",))
        p.dve(lambda e, o=giL[:, 0:TT], d0=MAGt[:, 0:TT], d1=xti[:, 0:TT]:
              e.tensor_tensor_scan(out=o, data0=d0, data1=d1, initial=0.0, op0=ALU.mult, op1=ALU.add),
              reads=("ga", "xti"), writes=("giL",))
        ce, se = csb[:, TT - 1:TT], snb[:, TT - 1:TT]
        gre, gie = grL[:, TT - 1:TT], giL[:, TT - 1:TT]
        p.dve(lambda e, o=sml(6), a=ce, bb=gre: e.tensor_mul(out=o, in0=a, in1=bb),
              reads=("cs", "grL"), writes=("sm6",))
        p.dve(lambda e, o=sml(11), a=se, bb=gie: e.tensor_mul(out=o, in0=a, in1=bb),
              reads=("sn", "giL"), writes=("sm11",))
        p.dve(lambda e, o=sml(6), a=sml(6), bb=sml(11): e.tensor_sub(out=o, in0=a, in1=bb),
              reads=("sm6", "sm11"), writes=("sm6",))
        p.dve(lambda e, o=sml(7), a=se, bb=gre: e.tensor_mul(out=o, in0=a, in1=bb),
              reads=("sn", "grL"), writes=("sm7",))
        p.dve(lambda e, o=sml(11), a=ce, bb=gie: e.tensor_mul(out=o, in0=a, in1=bb),
              reads=("cs", "giL"), writes=("sm11",))
        p.dve(lambda e, o=sml(7), a=sml(7), bb=sml(11): e.tensor_add(out=o, in0=a, in1=bb),
              reads=("sm7", "sm11"), writes=("sm7",))
        p.dve(lambda e, o=grL[:, TT:T], d0=MAGt[:, TT:T], d1=xtr[:, TT:T], ini=sml(6):
              e.tensor_tensor_scan(out=o, data0=d0, data1=d1, initial=ini, op0=ALU.mult, op1=ALU.add),
              reads=("ga", "xtr", "sm6"), writes=("grL",))
        p.dve(lambda e, o=giL[:, TT:T], d0=MAGt[:, TT:T], d1=xti[:, TT:T], ini=sml(7):
              e.tensor_tensor_scan(out=o, data0=d0, data1=d1, initial=ini, op0=ALU.mult, op1=ALU.add),
              reads=("ga", "xti", "sm7"), writes=("giL",))
        p.dve(lambda e, o=sml(8), i=sml(5):
              e.tensor_scalar(out=o, in0=i, scalar1=1.0 / (2 * PI), scalar2=MAGIC, op0=ALU.mult, op1=ALU.add),
              reads=("sm5",), writes=("sm8",))
        p.dve(lambda e, o=sml(8), i=sml(8): e.tensor_scalar(out=o, in0=i, scalar1=-MAGIC, scalar2=None, op0=ALU.add),
              reads=("sm8",), writes=("sm8",))
        p.dve(lambda e, o=sml(9), i=sml(8), t=sml(5):
              e.scalar_tensor_tensor(out=o, in0=i, scalar=-2 * PI, in1=t, op0=ALU.mult, op1=ALU.add),
              reads=("sm8", "sm5"), writes=("sm9",))
        p.dve(lambda e, o=sml(9), i=sml(9):
              e.tensor_scalar(out=o, in0=i, scalar1=-PI, scalar2=PI, op0=ALU.max, op1=ALU.min),
              reads=("sm9",), writes=("sm9",))
        p.dve(lambda e, o=sml(10), i=sml(9):
              e.tensor_scalar(out=o, in0=i, scalar1=PI / 2, scalar2=-2 * PI, op0=ALU.is_gt, op1=ALU.mult),
              reads=("sm9",), writes=("sm10",))
        p.dve(lambda e, o=sml(10), i=sml(10), r=sml(9):
              e.scalar_tensor_tensor(out=o, in0=i, scalar=PI / 2, in1=r, op0=ALU.add, op1=ALU.add),
              reads=("sm9", "sm10"), writes=("sm10",))
        p.dve(lambda e, o=sml(10), i=sml(10):
              e.tensor_scalar(out=o, in0=i, scalar1=-PI, scalar2=PI, op0=ALU.max, op1=ALU.min),
              reads=("sm10",), writes=("sm10",), inc=s_dv)
        p.wait("act", s_dv)
        p.op("act", lambda e, o=sml(12), i=sml(9): e.activation(out=o, in_=i, func=AF.Sin),
             reads=("sm9",), writes=("sm12",))
        p.op("act", lambda e, o=sml(13), i=sml(10): e.activation(out=o, in_=i, func=AF.Sin),
             reads=("sm10",), writes=("sm13",))
        p.op("act", lambda e, o=sml(14), i=sml(4): e.activation(out=o, in_=i, func=AF.Exp), inc=s_ac,
             reads=("sm4",), writes=("sm14",))
        p.wait("dve", s_ac)
        p.dve(lambda e, o=S_t[:, 0:1], a=sml(14), bb=sml(13): e.tensor_mul(out=o, in0=a, in1=bb),
              reads=("sm14", "sm13"), writes=("S_t",))
        p.dve(lambda e, o=S_t[:, 1:2], a=sml(14), bb=sml(12): e.tensor_mul(out=o, in0=a, in1=bb),
              reads=("sm14", "sm12"), writes=("S_t",))
        ce1, se1 = csb[:, T - 1:T], snb[:, T - 1:T]
        gre1, gie1 = grL[:, T - 1:T], giL[:, T - 1:T]
        p.dve(lambda e, o=sml(11), a=ce1, bb=gre1: e.tensor_mul(out=o, in0=a, in1=bb),
              reads=("cs", "grL"), writes=("sm11",))
        p.dve(lambda e, o=sml(12), a=se1, bb=gie1: e.tensor_mul(out=o, in0=a, in1=bb),
              reads=("sn", "giL"), writes=("sm12",))
        p.dve(lambda e, o=S_t[:, 2:3], a=sml(11), bb=sml(12): e.tensor_sub(out=o, in0=a, in1=bb),
              reads=("sm11", "sm12"), writes=("S_t",))
        p.dve(lambda e, o=sml(11), a=se1, bb=gre1: e.tensor_mul(out=o, in0=a, in1=bb),
              reads=("sn", "grL"), writes=("sm11",))
        p.dve(lambda e, o=sml(12), a=ce1, bb=gie1: e.tensor_mul(out=o, in0=a, in1=bb),
              reads=("cs", "giL"), writes=("sm12",))
        p.dve(lambda e, o=S_t[:, 3:4], a=sml(11), bb=sml(12): e.tensor_add(out=o, in0=a, in1=bb),
              reads=("sm11", "sm12"), writes=("S_t",), inc=s_dv)
        p.wait("sp", s_dv)
        p.op("sp", lambda e, o=cc_in[b], i=S_t[:]: e.dma_start(out=o, in_=i), inc=s_ex, inck=16)
        p.wait("gp", s_ex)
        p.op("gp", lambda e, i=cc_in[b], o=cc_out[b]: e.collective_compute(
            "AllGather", ALU.bypass, replica_groups=[list(range(NCORES))], ins=[i], outs=[o]),
            inc=s_cc)
        p.wait("sp", s_cc)
        for r in range(NCORES):
            p.op("sp", lambda e, o=G_t[:, r * 4:(r + 1) * 4], i=cc_out[b][r]: e.dma_start(out=o, in_=i),
                 inc=s_ex, inck=16)
        p.wait("dve", s_ex)
        for r in range(NCORES):
            Ar = G_t[:, r * 4 + 0:r * 4 + 1]
            Ai = G_t[:, r * 4 + 1:r * 4 + 2]
            Xr = G_t[:, r * 4 + 2:r * 4 + 3]
            Xi = G_t[:, r * 4 + 3:r * 4 + 4]
            if r == 0:
                Hpr, Hpi = zeros_h[:, 0:1], zeros_h[:, 1:2]
            else:
                Hpr, Hpi = Hall[:, r - 1:r], Hall[:, 8 + r - 1:8 + r]
            p.dve(lambda e, o=sml(8), a=Ar, bb=Hpr: e.tensor_mul(out=o, in0=a, in1=bb),
                  reads=("G", "Hall"), writes=("sm8",))
            p.dve(lambda e, o=sml(9), a=Ai, bb=Hpi: e.tensor_mul(out=o, in0=a, in1=bb),
                  reads=("G", "Hall"), writes=("sm9",))
            p.dve(lambda e, o=sml(8), a=sml(8), bb=sml(9): e.tensor_sub(out=o, in0=a, in1=bb),
                  reads=("sm8", "sm9"), writes=("sm8",))
            p.dve(lambda e, o=Hall[:, r:r + 1], a=sml(8), bb=Xr: e.tensor_add(out=o, in0=a, in1=bb),
                  reads=("sm8", "G"), writes=("Hall",))
            p.dve(lambda e, o=sml(8), a=Ar, bb=Hpi: e.tensor_mul(out=o, in0=a, in1=bb),
                  reads=("G", "Hall"), writes=("sm8",))
            p.dve(lambda e, o=sml(9), a=Ai, bb=Hpr: e.tensor_mul(out=o, in0=a, in1=bb),
                  reads=("G", "Hall"), writes=("sm9",))
            p.dve(lambda e, o=sml(8), a=sml(8), bb=sml(9): e.tensor_add(out=o, in0=a, in1=bb),
                  reads=("sm8", "sm9"), writes=("sm8",))
            p.dve(lambda e, o=Hall[:, 8 + r:8 + r + 1], a=sml(8), bb=Xi: e.tensor_add(out=o, in0=a, in1=bb),
                  reads=("sm8", "G"), writes=("Hall",))
        p.dve(lambda e, o=G_t[:, 0:8], a=Hall[:, 0:8], bb=oh8b[:]: e.tensor_mul(out=o, in0=a, in1=bb),
              reads=("Hall", "oh8b"), writes=("G",))
        p.dve(lambda e, o=sml(6), i=G_t[:, 0:8]: e.reduce_sum(out=o, in_=i, axis=AXX),
              reads=("G",), writes=("sm6",))
        p.dve(lambda e, o=G_t[:, 0:8], a=Hall[:, 8:16], bb=oh8b[:]: e.tensor_mul(out=o, in0=a, in1=bb),
              reads=("Hall", "oh8b"), writes=("G",))
        p.dve(lambda e, o=sml(7), i=G_t[:, 0:8]: e.reduce_sum(out=o, in_=i, axis=AXX),
              reads=("G",), writes=("sm7",))
        p.dve(lambda e, o=gpr[:, 0:TT], d0=MAGt[:, 0:TT], d1=zeros_h[:], ini=sml(6):
              e.tensor_tensor_scan(out=o, data0=d0, data1=d1, initial=ini, op0=ALU.mult, op1=ALU.add),
              reads=("ga", "zeros_h", "sm6"), writes=("gpr",))
        p.dve(lambda e, o=gpi[:, 0:TT], d0=MAGt[:, 0:TT], d1=zeros_h[:], ini=sml(7):
              e.tensor_tensor_scan(out=o, data0=d0, data1=d1, initial=ini, op0=ALU.mult, op1=ALU.add),
              reads=("ga", "zeros_h", "sm7"), writes=("gpi",))
        gre2, gie2 = gpr[:, TT - 1:TT], gpi[:, TT - 1:TT]
        p.dve(lambda e, o=sml(6), a=ce, bb=gre2: e.tensor_mul(out=o, in0=a, in1=bb),
              reads=("cs", "gpr"), writes=("sm6",))
        p.dve(lambda e, o=sml(11), a=se, bb=gie2: e.tensor_mul(out=o, in0=a, in1=bb),
              reads=("sn", "gpi"), writes=("sm11",))
        p.dve(lambda e, o=sml(6), a=sml(6), bb=sml(11): e.tensor_sub(out=o, in0=a, in1=bb),
              reads=("sm6", "sm11"), writes=("sm6",))
        p.dve(lambda e, o=sml(7), a=se, bb=gre2: e.tensor_mul(out=o, in0=a, in1=bb),
              reads=("sn", "gpr"), writes=("sm7",))
        p.dve(lambda e, o=sml(11), a=ce, bb=gie2: e.tensor_mul(out=o, in0=a, in1=bb),
              reads=("cs", "gpi"), writes=("sm11",))
        p.dve(lambda e, o=sml(7), a=sml(7), bb=sml(11): e.tensor_add(out=o, in0=a, in1=bb),
              reads=("sm7", "sm11"), writes=("sm7",))
        p.dve(lambda e, o=gpr[:, TT:T], d0=MAGt[:, TT:T], d1=zeros_h[:], ini=sml(6):
              e.tensor_tensor_scan(out=o, data0=d0, data1=d1, initial=ini, op0=ALU.mult, op1=ALU.add),
              reads=("ga", "zeros_h", "sm6"), writes=("gpr",))
        p.dve(lambda e, o=gpi[:, TT:T], d0=MAGt[:, TT:T], d1=zeros_h[:], ini=sml(7):
              e.tensor_tensor_scan(out=o, data0=d0, data1=d1, initial=ini, op0=ALU.mult, op1=ALU.add),
              reads=("ga", "zeros_h", "sm7"), writes=("gpi",))
        p.dve(lambda e, o=M1, a=grL[:], bb=gpr[:]: e.tensor_add(out=o, in0=a, in1=bb),
              reads=("grL", "gpr"), writes=("m1",))
        p.dve(lambda e, o=M2, a=giL[:], bb=gpi[:]: e.tensor_add(out=o, in0=a, in1=bb),
              reads=("giL", "gpi"), writes=("m2",))
        p.dve(lambda e, o=THm, a=csb[:], bb=M1: e.tensor_mul(out=o, in0=a, in1=bb),
              reads=("cs", "m1"), writes=("th",))
        p.dve(lambda e, o=PRE, a=snb[:], bb=M2: e.tensor_mul(out=o, in0=a, in1=bb),
              reads=("sn", "m2"), writes=("asq",))
        p.dve(lambda e, o=h_k(b), a=THm, bb=PRE: e.tensor_sub(out=o, in0=a, in1=bb),
              reads=("th", "asq"), writes=("h",))
        p.dve(lambda e, o=THm, a=snb[:], bb=M1: e.tensor_mul(out=o, in0=a, in1=bb),
              reads=("sn", "m1"), writes=("th",))
        p.dve(lambda e, o=PRE, a=csb[:], bb=M2: e.tensor_mul(out=o, in0=a, in1=bb),
              reads=("cs", "m2"), writes=("asq",))
        p.dve(lambda e, o=h_k(8 + b), a=THm, bb=PRE: e.tensor_add(out=o, in0=a, in1=bb),
              reads=("th", "asq"), writes=("h",), inc=s_dv)
    h_ready_dv = s_dv.n
    tap("h", HREG[:, 0:NK * 1024], (128, NK * 1024), F16)

    p.wait("pe", s_dv, h_ready_dv)
    p.wait("pe", s_ac)
    for g in range(8):
        banks = [ps[(g % 2) * 4 + i] for i in range(4)]
        lds = [pool_load(poolA, k, W_out[k * 128:(k + 1) * 128, g * 256:(g + 1) * 256], 256)
               for k in range(NK)]
        if g >= 2:
            p.wait("pe", s_ac)
        for k in range(NK):
            sl = poolA.slot(k)
            p.wait("pe", lds[k][0], lds[k][1])
            for j in range(2):
                for tt in range(NTT):
                    p.op("pe", lambda e, o=banks[j * 2 + tt][:], w=sl[:, j * 128:(j + 1) * 128],
                         r=h_k(k, tt), st=(k == 0), fin=(k == NK - 1):
                         e.matmul(out=o, lhsT=w, rhs=r, start=st, stop=fin))
        p.milestone("pe", s_pe)
        for k in range(NK):
            poolA.guard[k] = s_pe.n
        p.wait("act", s_pe)
        for j in range(2):
            oc = g * 2 + j
            for tt in range(NTT):
                p.op("act", lambda e, o=ACC_k(oc, tt), i=banks[j * 2 + tt][:], bb=bcol_out[:, oc:oc + 1]:
                     e.activation(out=o, in_=i, func=AF.Identity, bias=bb), inc=s_ac,
                     reads=("bcol_out",), writes=("acc",))
    tap("y_attn", XH[:], (128, NK * 1024), F16)

    for half in range(2):
        for og in range(11):
            oc0 = half * 22 + og * 2
            gb = [ps[(og % 2) * 4 + i] for i in range(4)]
            lds = [pool_load(poolA, k, sw_gate[k * 128:(k + 1) * 128, oc0 * 128:(oc0 + 2) * 128], 256)
                   for k in range(NK)]
            p.wait("pe", s_dv)
            p.wait("pe", s_ac)
            for k in range(NK):
                sl = poolA.slot(k)
                p.wait("pe", lds[k][0], lds[k][1])
                for j in range(2):
                    for tt in range(NTT):
                        p.op("pe", lambda e, o=gb[j * 2 + tt][:], w=sl[:, j * 128:(j + 1) * 128],
                             r=xnT_k(k, tt), st=(k == 0), fin=(k == NK - 1):
                             e.matmul(out=o, lhsT=w, rhs=r, start=st, stop=fin))
            p.milestone("pe", s_pe)
            for k in range(NK):
                poolA.guard[k] = s_pe.n
            gate_pe = s_pe.n
            ub = [ps[((og + 1) % 2) * 4 + i] for i in range(4)]
            lds = [pool_load(poolB, k, sw_up[k * 128:(k + 1) * 128, oc0 * 128:(oc0 + 2) * 128], 256)
                   for k in range(NK)]
            p.wait("pe", s_dv)
            for k in range(NK):
                sl = poolB.slot(k)
                p.wait("pe", lds[k][0], lds[k][1])
                for j in range(2):
                    for tt in range(NTT):
                        p.op("pe", lambda e, o=ub[j * 2 + tt][:], w=sl[:, j * 128:(j + 1) * 128],
                             r=xnT_k(k, tt), st=(k == 0), fin=(k == NK - 1):
                             e.matmul(out=o, lhsT=w, rhs=r, start=st, stop=fin))
            p.milestone("pe", s_pe)
            for k in range(NK):
                poolB.guard[k] = s_pe.n
            up_pe = s_pe.n
            p.wait("act", s_pe, gate_pe)
            gs_sl = {}
            for j in range(2):
                for tt in range(NTT):
                    base = j * NTT + tt
                    slc = scr(base // 2, 1024)[:, (base % 2) * TT:(base % 2) * TT + TT]
                    gs_sl[(j, tt)] = slc
                    p.op("act", lambda e, o=slc, i=gb[j * 2 + tt][:]:
                         e.activation(out=o, in_=i, func=AF.Silu), inc=s_ac, writes=("gs",))
            p.wait("dve", s_ac)
            p.wait("dve", s_pe, up_pe)
            for j in range(2):
                li = og * 2 + j
                for tt in range(NTT):
                    tmp = scr(3, 1024)[:, (tt) * TT:(tt) * TT + TT]
                    p.dve(lambda e, o=tmp, a=ub[j * 2 + tt][:], bb=gate_b[:, tt * TT:(tt + 1) * TT]:
                          e.tensor_mul(out=o, in0=a, in1=bb), reads=("gate_b",), writes=("tmp",))
                    p.dve(lambda e, o=sh_k(li, tt), a=gs_sl[(j, tt)], bb=tmp:
                          e.tensor_mul(out=o, in0=a, in1=bb), reads=("tmp", "gs"), writes=("sh",))
            p.milestone("dve", s_dv)
        sh_dv = s_dv.n
        for ocp in range(8):
            banks = [ps[(ocp % 2) * 4 + i] for i in range(4)]
            wd_ms = []

            def issue(i, ocp=ocp, half=half, wd_ms=wd_ms):
                wd_ms.append(pool_load(
                    poolC, i, sw_down[(half * 22 + i) * 128:(half * 22 + i + 1) * 128,
                                      ocp * 256:(ocp + 1) * 256]))

            for i in range(4):
                issue(i)
            for i in range(22):
                p.wait("pe", wd_ms[i][0], wd_ms[i][1])
                if i == 0:
                    p.wait("pe", s_dv, sh_dv)
                    p.wait("pe", s_dv)
                sl = poolC.slot(i)
                for j in range(2):
                    for tt in range(NTT):
                        p.op("pe", lambda e, o=banks[j * 2 + tt][:], w=sl[:, j * 128:(j + 1) * 128],
                             r=sh_k(i, tt), st=(i == 0), fin=(i == 21):
                             e.matmul(out=o, lhsT=w, rhs=r, start=st, stop=fin))
                p.milestone("pe", s_pe)
                poolC.guard[i % 8] = s_pe.n
                if i + 4 < 22:
                    issue(i + 4)
            p.wait("dve", s_pe)
            for j in range(2):
                oc = ocp * 2 + j
                for tt in range(NTT):
                    p.dve(lambda e, o=ACC_k(oc, tt), a=ACC_k(oc, tt), bb=banks[j * 2 + tt][:]:
                          e.tensor_add(out=o, in0=a, in1=bb), reads=("acc",), writes=("acc",))
            p.milestone("dve", s_dv)
    tap("acc_shared", XH[:], (128, NK * 1024), F16)

    igroups = [(0, 3), (3, 3), (6, 3), (9, 2)]
    for ex in range(E):
        par = ex % 2
        CBC = scr(4)
        for tt in range(NTT):
            bank = ps[6]
            p.wait("pe", s_dv)
            p.op("pe", lambda e, o=bank[:], r=Crows[0:8, tt * TT:(tt + 1) * TT], x_=ex:
                 e.matmul(out=o, lhsT=e8sel[0:8, x_ * 128:(x_ + 1) * 128], rhs=r, start=True, stop=True), inc=s_pe)
            p.wait("dve", s_pe)
            p.dve(lambda e, o=CBC[:, tt * TT:(tt + 1) * TT], i=bank[:]: e.tensor_copy(out=o, in_=i),
                  writes=("cbc",), inc=s_dv)
        for (i0, ni) in igroups:
            gbanks = [ps[i] for i in range(2 * ni)]
            ldg = [pool_load(poolA, k, w_gate[ex, k * 128:(k + 1) * 128, i0 * 128:(i0 + ni) * 128], ni * 128)
                   for k in range(NK)]
            ldu = [pool_load(poolB, k, w_up[ex, k * 128:(k + 1) * 128, i0 * 128:(i0 + ni) * 128], ni * 128)
                   for k in range(NK)]
            p.wait("pe", s_ac)
            p.wait("pe", s_dv)
            for k in range(NK):
                slg = poolA.slot(k)
                p.wait("pe", ldg[k][0], ldg[k][1])
                for j in range(ni):
                    for tt in range(NTT):
                        p.op("pe", lambda e, o=gbanks[j * 2 + tt][:], w=slg[:, j * 128:(j + 1) * 128],
                             r=xnT_k(k, tt), st=(k == 0), fin=(k == NK - 1):
                             e.matmul(out=o, lhsT=w, rhs=r, start=st, stop=fin))
            p.milestone("pe", s_pe)
            gate_pe = s_pe.n
            for k in range(NK):
                poolA.guard[k] = gate_pe
            p.wait("act", s_pe, gate_pe)
            gs_sl = {}
            for j in range(ni):
                for tt in range(NTT):
                    base = j * NTT + tt
                    slc = scr(base // 2, 1024)[:, (base % 2) * TT:(base % 2) * TT + TT]
                    gs_sl[(j, tt)] = slc
                    p.op("act", lambda e, o=slc, i=gbanks[j * 2 + tt][:]:
                         e.activation(out=o, in_=i, func=AF.Silu), inc=s_ac, writes=("gs",))
            silu_ac = s_ac.n
            p.wait("pe", s_ac, silu_ac)
            for k in range(NK):
                slu = poolB.slot(k)
                p.wait("pe", ldu[k][0], ldu[k][1])
                for j in range(ni):
                    for tt in range(NTT):
                        p.op("pe", lambda e, o=gbanks[j * 2 + tt][:], w=slu[:, j * 128:(j + 1) * 128],
                             r=xnT_k(k, tt), st=(k == 0), fin=(k == NK - 1):
                             e.matmul(out=o, lhsT=w, rhs=r, start=st, stop=fin))
            p.milestone("pe", s_pe)
            up_pe = s_pe.n
            for k in range(NK):
                poolB.guard[k] = up_pe
            p.wait("dve", s_pe, up_pe)
            p.wait("dve", s_ac, silu_ac)
            for j in range(ni):
                for tt in range(NTT):
                    tmp = scr(3, 1024)[:, tt * TT:tt * TT + TT]
                    p.dve(lambda e, o=tmp, a=gbanks[j * 2 + tt][:], bb=CBC[:, tt * TT:(tt + 1) * TT]:
                          e.tensor_mul(out=o, in0=a, in1=bb), reads=("cbc",), writes=("tmp",))
                    p.dve(lambda e, o=hp_k(par, i0 + j, tt), a=gs_sl[(j, tt)], bb=tmp:
                          e.tensor_mul(out=o, in0=a, in1=bb), reads=("tmp", "gs"), writes=("hp",))
            p.milestone("dve", s_dv)
        hp_dv = s_dv.n
        for ocp in range(8):
            banks = [ps[(ocp % 2) * 4 + i] for i in range(4)]
            wd_ms = []

            def issue(i, ex=ex, ocp=ocp, wd_ms=wd_ms):
                wd_ms.append(pool_load(
                    poolC, i, w_down[ex, i * 128:(i + 1) * 128, ocp * 256:(ocp + 1) * 256]))

            for i in range(4):
                issue(i)
            for i in range(NI):
                p.wait("pe", wd_ms[i][0], wd_ms[i][1])
                if i == 0:
                    p.wait("pe", s_dv, hp_dv)
                    p.wait("pe", s_dv)
                sl = poolC.slot(i)
                for j in range(2):
                    for tt in range(NTT):
                        p.op("pe", lambda e, o=banks[j * 2 + tt][:], w=sl[:, j * 128:(j + 1) * 128],
                             r=hp_k(par, i, tt), st=(i == 0), fin=(i == NI - 1):
                             e.matmul(out=o, lhsT=w, rhs=r, start=st, stop=fin))
                p.milestone("pe", s_pe)
                poolC.guard[i % 8] = s_pe.n
                if i + 4 < NI:
                    issue(i + 4)
            p.wait("dve", s_pe)
            for j in range(2):
                oc = ocp * 2 + j
                for tt in range(NTT):
                    p.dve(lambda e, o=ACC_k(oc, tt), a=ACC_k(oc, tt), bb=banks[j * 2 + tt][:]:
                          e.tensor_add(out=o, in0=a, in1=bb), reads=("acc",), writes=("acc",))
            p.milestone("dve", s_dv)
    acc_done_dv = s_dv.n
    tap("acc_final", XH[:], (128, NK * 1024), F16)

    p.wait("pe", s_dv, acc_done_dv)
    p.wait("sp", s_dv, acc_done_dv)
    st_done = []
    for tj in range(8):
        xsl = scr(4 * (tj % 2), 2048)
        osl = scr(4 * (tj % 2) + 2, 2048)
        if tj >= 2:
            p.wait("sp", s_st)
        sdma(xsl, x_hbm[tj * 128:(tj + 1) * 128, :], sem=s_x2[tj % 2])
        xld = s_x2[tj % 2].n
        if tj >= 2:
            p.wait("pe", s_dv)
        for q in range(4):
            bank = ps[q]
            for j in range(4):
                k = q * 4 + j
                p.op("pe", lambda e, o=bank[:, j * 128:(j + 1) * 128],
                     w=XH[:, k * 1024 + tj * 128:k * 1024 + (tj + 1) * 128]:
                     e.matmul(out=o, lhsT=w, rhs=ident16[:], start=True, stop=True))
            p.milestone("pe", s_pe)
            p.wait("dve", s_pe)
            p.wait("dve", s_x2[tj % 2], xld)
            if tj >= 2:
                p.wait("dve", s_st)
            p.dve(lambda e, o=osl[:, q * 512:(q + 1) * 512], a=bank[:, 0:512],
                  bb=xsl[:, q * 512:(q + 1) * 512]:
                  e.tensor_add(out=o, in0=a, in1=bb), reads=("xre",), writes=("outtm",))
        p.milestone("dve", s_dv)
        p.wait("pe", s_dv)
        p.wait("sp", s_dv)
        p.op("sp", lambda e, o=out_h[tj * 128:(tj + 1) * 128, :], i=osl: e.dma_start(out=o, in_=i),
             inc=s_st, inck=16)
        st_done.append(s_st.n)
    p.wait("sp", s_st)

    with es:
        def replay(e, lst):
            for ent in lst:
                ins = ent["fn"](e)
                for (h, k) in (ent["incs"] or ()):
                    ins.then_inc(h, k)

        @block.sync
        def _(e):
            replay(e, p.q["sp"])

        @block.tensor
        def _(e):
            replay(e, p.q["pe"])

        @block.vector
        def _(e):
            replay(e, p.q["dve"])

        @block.scalar
        def _(e):
            replay(e, p.q["act"])

        @block.gpsimd
        def _(e):
            replay(e, p.q["gp"])

    return nc


_IDENT = np.eye(128, dtype=np.float32)
_E8SEL = np.zeros((8, 1024), dtype=np.float32)
for _e in range(8):
    _E8SEL[_e, _e * 128:(_e + 1) * 128] = 1.0
_ONES = np.ones((1, 128), dtype=np.float32)


def kernel(L0_attn_x, L0_attn_a_real_param, L0_attn_a_imag_param, L0_attn_W_in,
           L0_attn_b_in, L0_attn_Wg_x, L0_attn_bg_x, L0_attn_Wg_a, L0_attn_bg_a,
           L0_attn_W_out, L0_attn_b_out, L0_ffn_router_w, L0_ffn_w_gate,
           L0_ffn_w_up, L0_ffn_w_down, L0_ffn_sw_gate, L0_ffn_sw_up,
           L0_ffn_sw_down, L0_ffn_shared_gate_w, _taps=(), _nc=None):
    nc = _nc if _nc is not None else build_program(taps=_taps)
    xf = np.ascontiguousarray(np.asarray(L0_attn_x, dtype=np.float32).reshape(NCORES * T, D))
    common = dict(
        W_in=np.ascontiguousarray(L0_attn_W_in, dtype=np.float32),
        b_in=np.ascontiguousarray(L0_attn_b_in, dtype=np.float32).reshape(1, D),
        Wg_x=np.ascontiguousarray(L0_attn_Wg_x, dtype=np.float32),
        bg_x=np.ascontiguousarray(L0_attn_bg_x, dtype=np.float32).reshape(1, DH),
        Wg_a=np.ascontiguousarray(L0_attn_Wg_a, dtype=np.float32),
        bg_a=np.ascontiguousarray(L0_attn_bg_a, dtype=np.float32).reshape(1, DH),
        W_out=np.ascontiguousarray(L0_attn_W_out, dtype=np.float32),
        b_out=np.ascontiguousarray(L0_attn_b_out, dtype=np.float32).reshape(1, D),
        a_real=np.ascontiguousarray(L0_attn_a_real_param, dtype=np.float32).reshape(1, DH),
        a_imag=np.ascontiguousarray(L0_attn_a_imag_param, dtype=np.float32).reshape(1, DH),
        router_w=np.ascontiguousarray(L0_ffn_router_w, dtype=np.float32),
        w_gate=np.ascontiguousarray(L0_ffn_w_gate, dtype=np.float32),
        w_up=np.ascontiguousarray(L0_ffn_w_up, dtype=np.float32),
        w_down=np.ascontiguousarray(L0_ffn_w_down, dtype=np.float32),
        sw_gate=np.ascontiguousarray(L0_ffn_sw_gate, dtype=np.float32),
        sw_up=np.ascontiguousarray(L0_ffn_sw_up, dtype=np.float32),
        sw_down=np.ascontiguousarray(L0_ffn_sw_down, dtype=np.float32),
        sgw=np.ascontiguousarray(L0_ffn_shared_gate_w, dtype=np.float32).reshape(1, D),
        ident=_IDENT,
        e8sel=_E8SEL,
    )
    in_maps = []
    for c in range(NCORES):
        oh = np.zeros((1, 8), dtype=np.float32)
        if c > 0:
            oh[0, c - 1] = 1.0
        m = dict(common)
        m["x"] = np.ascontiguousarray(xf[c * T:(c + 1) * T])
        m["oh8"] = oh
        in_maps.append(m)
    res = run_bass_kernel_spmd(nc, in_maps, core_ids=list(range(NCORES)))
    out = np.concatenate([r["out"] for r in res.results], axis=0)
    main = out.reshape(1, NCORES * T, D).astype(np.float32)
    if _taps:
        return main, res
    return main


# version 04 (final)
# speedup vs baseline: 1.1705x; 1.1705x over previous
import math
import numpy as np

import concourse.bass as bass
import concourse.mybir as mybir
from concourse.bass_utils import run_bass_kernel_spmd

F32 = mybir.dt.float32
F16 = mybir.dt.float16
AF = mybir.ActivationFunctionType
ALU = mybir.AluOpType
AXX = mybir.AxisListType.X

NCORES = 8
T = 1024
TT = 512
NTT = 2
D = 2048
NK = 16
DH = 1024
NB = 8
E = 8
I_DIM = 1408
NI = 11
SI = 5632
PI = math.pi
MAGIC = 12582912.0

ENGINES = ("sp", "pe", "dve", "act", "gp")


class Sem:
    def __init__(self, h):
        self.h = h
        self.n = 0


class Prog:
    TRACKED = ("dve", "act")

    def __init__(self, nc):
        self.nc = nc
        self.q = {e: [] for e in ENGINES}
        self.pw = {e: set() for e in self.TRACKED}
        self.pr = {e: set() for e in self.TRACKED}

    def wait(self, eng, sem, val=None):
        v = sem.n if val is None else val
        if v <= 0:
            return
        self.q[eng].append({"fn": (lambda e, h=sem.h, vv=v: e.wait_ge(h, vv)), "incs": None})

    def op(self, eng, fn, inc=None, inck=1, reads=(), writes=()):
        if eng in self.TRACKED:
            rd, wr = set(reads), set(writes)
            if (rd & self.pw[eng]) or (wr & (self.pw[eng] | self.pr[eng])):
                self.q[eng].append({"fn": (lambda e: e.drain()), "incs": None})
                self.pw[eng].clear()
                self.pr[eng].clear()
            self.pw[eng] |= wr
            self.pr[eng] |= rd
        ent = {"fn": fn, "incs": []}
        if inc is not None:
            ent["incs"].append((inc.h, inck))
            inc.n += inck
        self.q[eng].append(ent)

    def milestone(self, eng, sem, k=1):
        for ent in reversed(self.q[eng]):
            if ent["incs"] is not None:
                ent["incs"].append((sem.h, k))
                sem.n += k
                return
        raise RuntimeError("no op to attach milestone to")

    def dve(self, fn, reads=(), writes=(), inc=None, inck=1):
        self.op("dve", fn, inc=inc, inck=inck, reads=reads, writes=writes)


class Pool:
    def __init__(self, tensor, nslots, width, sems):
        self.t = tensor
        self.n = nslots
        self.w = width
        self.guard = [0] * nslots
        self.sem = sems

    def slot(self, i):
        j = i % self.n
        return self.t[:, j * self.w:(j + 1) * self.w]


def build_program(taps=()):
    taps = set(taps)
    nc = bass.Bass("TRN2", target_bir_lowering=False, debug=False)

    def din(name, shape):
        return nc.dram_tensor(name, list(shape), F32, kind="ExternalInput").ap()

    x_hbm = din("x", (T, D))
    W_in = din("W_in", (D, D))
    b_in = din("b_in", (1, D))
    Wg_x = din("Wg_x", (NB, 256, 128))
    bg_x = din("bg_x", (1, DH))
    Wg_a = din("Wg_a", (NB, 256, 128))
    bg_a = din("bg_a", (1, DH))
    W_out = din("W_out", (D, D))
    b_out = din("b_out", (1, D))
    a_real = din("a_real", (1, DH))
    a_imag = din("a_imag", (1, DH))
    router_w = din("router_w", (E, D))
    w_gate = din("w_gate", (E, D, I_DIM))
    w_up = din("w_up", (E, D, I_DIM))
    w_down = din("w_down", (E, I_DIM, D))
    sw_gate = din("sw_gate", (D, SI))
    sw_up = din("sw_up", (D, SI))
    sw_down = din("sw_down", (SI, D))
    sgw = din("sgw", (1, D))
    ident_in = din("ident", (128, 128))
    oh8_in = din("oh8", (1, 8))
    e8sel_in = din("e8sel", (8, 1024))

    out_h = nc.dram_tensor("out", [T, D], F32, kind="ExternalOutput").ap()
    cc_in = [nc.dram_tensor(f"cc_in{b}", [128, 4], F32, kind="Internal").ap() for b in range(NB)]
    cc_out = [nc.dram_tensor(f"cc_out{b}", [NCORES, 128, 4], F32, kind="Internal").ap() for b in range(NB)]

    import contextlib
    es = contextlib.ExitStack()

    def sb(name, shape, dt):
        return es.enter_context(nc.sbuf_tensor(name, list(shape), dt))

    SCR = sb("SCR", (128, 8 * 1024), F32)

    def scr(i, n=1024):
        return SCR[:, i * 1024:i * 1024 + n]

    gate_b = sb("gate_b", (128, T), F16)
    ident32 = sb("ident32", (128, 128), F32)
    ident16 = sb("ident16", (128, 128), F16)
    smalls = sb("smalls", (128, 64), F32)

    def sml(i, n=1):
        return smalls[:, i:i + n]

    bcol_in = sb("bcol_in", (128, NK), F32)
    bcol_out = sb("bcol_out", (128, NK), F32)
    bcol_gx = sb("bcol_gx", (128, NB), F32)
    bcol_ga = sb("bcol_ga", (128, NB), F32)
    sp_col = sb("sp_col", (128, NB), F32)
    ai_col = sb("ai_col", (128, NB), F32)
    rows_t = sb("rows_t", (128, D), F32)
    Crows = rows_t[32:40, 0:T]
    e8sel = rows_t[32:40, T:T + 1024]
    rw_s = rows_t[64:72, :]
    grow = rows_t[0:1, T:T + T]
    ones_t = sb("ones_t", (128, 128), F32)
    G_t = sb("G_t", (128, 32), F32)
    Hall = sb("Hall", (128, 16), F32)
    S_t = sb("S_t", (128, 4), F32)
    oh8b = sb("oh8b", (128, 8), F32)

    xnT = sb("xnT", (128, NK * 1024), F16)

    def xnT_k(k, tt=None):
        if tt is None:
            return xnT[:, k * 1024:(k + 1) * 1024]
        return xnT[:, k * 1024 + tt * TT:k * 1024 + tt * TT + TT]

    XH = sb("XH", (128, NK * 1024), F16)

    def XH_k(k, tt=None):
        if tt is None:
            return XH[:, k * 1024:(k + 1) * 1024]
        return XH[:, k * 1024 + tt * TT:k * 1024 + tt * TT + TT]

    ACC_k = XH_k
    HREG = sb("HREG", (128, 22 * 1024), F16)

    def h_k(k, tt=None):
        if tt is None:
            return HREG[:, k * 1024:(k + 1) * 1024]
        return HREG[:, k * 1024 + tt * TT:k * 1024 + tt * TT + TT]

    sh_k = h_k

    def hp_k(par, i, tt=None):
        base = (par * NI + i) * 1024
        if tt is None:
            return HREG[:, base:base + 1024]
        return HREG[:, base + tt * TT:base + tt * TT + TT]

    WPA = sb("WPA", (128, 16 * 384), F16)
    WPB = sb("WPB", (128, 16 * 384), F16)
    WPC = sb("WPC", (128, 8 * 256), F16)
    wg16 = sb("wg16", (128, 2 * NB * 2 * 128), F16)

    def wg_tile(g, b, kk):
        off = ((g * NB + b) * 2 + kk) * 128
        return wg16[:, off:off + 128]

    rwT = sb("rwT", (128, NK * 8), F16)
    sgw_col = sb("sgw_col", (128, NK), F16)
    csb = sb("csb", (128, T), F16)
    snb = sb("snb", (128, T), F16)
    xtr = sb("xtr", (128, T), F16)
    xti = sb("xti", (128, T), F16)
    grL = sb("grL", (128, T), F16)
    giL = sb("giL", (128, T), F16)
    gpr = sb("gpr", (128, T), F16)
    gpi = sb("gpi", (128, T), F16)
    zeros_h = sb("zeros_h", (128, TT), F16)

    ps = [es.enter_context(nc.psum_tensor(f"ps{i}", [128, TT], F32)) for i in range(8)]

    s_wg = Sem(es.enter_context(nc.semaphore("s_wg")))
    s_ld = Sem(es.enter_context(nc.semaphore("s_ld")))
    s_x2 = [Sem(es.enter_context(nc.semaphore(f"s_x{i}"))) for i in range(2)]
    semA = [Sem(es.enter_context(nc.semaphore(f"sA{i}"))) for i in range(16)]
    semB = [Sem(es.enter_context(nc.semaphore(f"sB{i}"))) for i in range(16)]
    semC = [Sem(es.enter_context(nc.semaphore(f"sC{i}"))) for i in range(8)]
    s_pe = Sem(es.enter_context(nc.semaphore("s_pe")))
    s_dv = Sem(es.enter_context(nc.semaphore("s_dv")))
    s_ac = Sem(es.enter_context(nc.semaphore("s_ac")))
    s_cc = Sem(es.enter_context(nc.semaphore("s_cc")))
    s_st = Sem(es.enter_context(nc.semaphore("s_st")))
    s_ex = Sem(es.enter_context(nc.semaphore("s_ex")))

    block = es.enter_context(nc.Block())
    p = Prog(nc)

    def tap(name, ap, shape, dt=None):
        if name not in taps:
            return
        th = nc.dram_tensor("tap_" + name, list(shape), dt or ap.dtype, kind="ExternalOutput").ap()
        for sem in (s_pe, s_dv, s_ac):
            p.wait("sp", sem)
        p.op("sp", lambda e, o=th, i=ap: e.dma_start(out=o, in_=i), inc=s_st, inck=16)
        for eng in ("pe", "dve", "act", "gp"):
            p.wait(eng, s_st)

    def sdma(dst, src, sem=s_ld):
        p.op("sp", lambda e, o=dst, i=src: e.dma_start(out=o, in_=i), inc=sem, inck=16)

    def wdma(dst, src, sem=None):
        sem = sem if sem is not None else s_wg
        p.op("gp", lambda e, o=dst, i=src: e.dma_start(out=o, in_=i), inc=sem, inck=16)

    def pool_load(pool, i, src, width=None):
        j = i % pool.n
        if pool.guard[j]:
            p.wait("gp", s_pe, pool.guard[j])
        w = width if width is not None else pool.w
        sl = pool.slot(i)[:, 0:w]
        sem = pool.sem[j]
        p.op("gp", lambda e, o=sl, i_=src: e.dma_start(out=o, in_=i_), inc=sem, inck=16)
        return sem, sem.n

    sdma(ident32[:], ident_in)
    sdma(e8sel[:], e8sel_in)
    wdma(ident16[:], ident_in)
    for g, W in ((0, Wg_x), (1, Wg_a)):
        for b in range(NB):
            for kk in range(2):
                wdma(wg_tile(g, b, kk), W[b, kk * 128:(kk + 1) * 128, :])
    wg_wd = s_wg.n

    p.dve(lambda e: e.memset(ones_t[:], 1.0), writes=("ones_t",))
    p.dve(lambda e: e.memset(zeros_h[:], 0.0), writes=("zeros_h",))

    vecs = [
        ("bcol_in", b_in, NK, bcol_in, None),
        ("bcol_out", b_out, NK, bcol_out, None),
        ("bcol_gx", bg_x, NB, bcol_gx, None),
        ("bcol_ga", bg_a, NB, bcol_ga, None),
        ("sp_col", a_real, NB, sp_col, "softplus"),
        ("ai_col", a_imag, NB, ai_col, None),
        ("sgw_col", sgw, NK, sgw_col, None),
    ]
    p.wait("pe", s_ld)
    pe_vec_done = []
    for vi, (vnm, src, nchunk, dstcol, post) in enumerate(vecs):
        rb = 64 * (vi % 2)
        row = rows_t[rb:rb + 1, :]
        if vi >= 2:
            p.wait("sp", s_pe, pe_vec_done[vi - 2])
        sdma(row[:, 0:128 * nchunk], src[:, 0:128 * nchunk], sem=s_x2[vi % 2])
        p.wait("pe", s_x2[vi % 2])
        if vi >= 2:
            p.wait("pe", s_ac)
        bank = ps[vi % 2]
        for c in range(nchunk):
            p.op("pe", lambda e, i=row[:, c * 128:(c + 1) * 128], o=bank[:, c:c + 1], b_=rb:
                 e.transpose(out=o, in_=i, identity=ident32[b_:b_ + 1, b_:b_ + 1]))
        p.milestone("pe", s_pe)
        pe_vec_done.append(s_pe.n)
        p.wait("act", s_pe)
        if post == "softplus":
            pcol = G_t[:, 8:16]
            mcol = G_t[:, 16:24]
            ncol = G_t[:, 24:32]
            ecol = G_t[:, 0:8]
            qcol = Hall[:, 0:8]
            lcol = Hall[:, 8:16]
            p.op("act", lambda e, o=pcol, i=bank[:, 0:nchunk]: e.copy(out=o, in_=i), inc=s_ac,
                 writes=("spp",))
            p.wait("dve", s_ac)
            p.dve(lambda e, o=mcol, i=pcol: e.tensor_scalar(out=o, in0=i, scalar1=0.0, scalar2=None, op0=ALU.max),
                  reads=("spp",), writes=("spm",))
            p.op("act", lambda e, o=ncol, i=pcol: e.activation(out=o, in_=i, func=AF.Abs), inc=s_ac,
                 reads=("spp",), writes=("spn",))
            p.op("act", lambda e, o=ecol, i=ncol: e.activation(out=o, in_=i, func=AF.Exp, scale=-1.0), inc=s_ac,
                 reads=("spn",), writes=("spe",))
            p.wait("dve", s_ac)
            p.dve(lambda e, o=qcol, i=ecol: e.tensor_scalar(out=o, in0=i, scalar1=1.0, scalar2=None, op0=ALU.add),
                  reads=("spe",), writes=("spq",), inc=s_dv)
            p.wait("act", s_dv)
            p.op("act", lambda e, o=lcol, i=qcol: e.activation(out=o, in_=i, func=AF.Ln), inc=s_ac,
                 reads=("spq",), writes=("spl",))
            p.wait("dve", s_ac)
            p.dve(lambda e, o=dstcol[:, 0:nchunk], a=mcol, bb=lcol: e.tensor_add(out=o, in0=a, in1=bb),
                  reads=("spm", "spl"), writes=("sp_col",), inc=s_dv)
            p.wait("act", s_dv)
        else:
            p.op("act", lambda e, o=dstcol[:, 0:nchunk], i=bank[:, 0:nchunk]:
                 e.copy(out=o, in_=i), inc=s_ac, writes=(vnm,))
    p.wait("sp", s_pe)
    sdma(rw_s[:], router_w)
    p.wait("pe", s_ld)
    for k in range(NK):
        bank = ps[2]
        if k > 0:
            p.wait("pe", s_ac)
        p.op("pe", lambda e, i=rw_s[0:8, k * 128:(k + 1) * 128], o=bank[:, 0:8]:
             e.transpose(out=o, in_=i, identity=ident32[64:72, 64:72]), inc=s_pe)
        p.wait("act", s_pe)
        p.op("act", lambda e, o=rwT[:, k * 8:(k + 1) * 8], i=bank[:, 0:8]:
             e.copy(out=o, in_=i), inc=s_ac, writes=("rwT",))
    row = rows_t[64:65, 0:8]
    p.wait("sp", s_pe)
    sdma(row, oh8_in)
    p.wait("pe", s_ld)
    p.wait("pe", s_ac)
    p.op("pe", lambda e, o=ps[3][:, 0:8], r=row: e.matmul(out=o, lhsT=ones_t[64:65, 0:128], rhs=r, start=True, stop=True),
         inc=s_pe)
    p.wait("dve", s_pe)
    p.dve(lambda e, o=oh8b[:], i=ps[3][:, 0:8]: e.tensor_copy(out=o, in_=i), writes=("oh8b",), inc=s_dv)
    p.wait("pe", s_dv)

    EPS = 1e-5
    ln_done_pe = []
    for tj in range(8):
        xsl = scr(4 * (tj % 2), 2048)
        xnsl = scr(4 * (tj % 2) + 2, 2048)
        xid = ("xtile", tj % 2)
        nid = ("xntile", tj % 2)
        if tj >= 2:
            p.wait("sp", s_pe, ln_done_pe[tj - 2])
        sdma(xsl, x_hbm[tj * 128:(tj + 1) * 128, :], sem=s_x2[tj % 2])
        p.wait("dve", s_x2[tj % 2])
        p.dve(lambda e, x=xsl, m=sml(0): e.reduce_sum(out=m, in_=x, axis=AXX),
              reads=(xid,), writes=("sm0",))
        p.dve(lambda e, m=sml(0): e.tensor_scalar(out=m, in0=m, scalar1=1.0 / D, scalar2=None, op0=ALU.mult),
              reads=("sm0",), writes=("sm0",))
        p.dve(lambda e, x=xsl, o=xnsl, m=sml(0):
              e.tensor_scalar(out=o, in0=x, scalar1=m, scalar2=None, op0=ALU.subtract),
              reads=(xid, "sm0"), writes=(nid,), inc=s_dv)
        p.wait("act", s_dv)
        p.op("act", lambda e, i=xnsl, o=xsl, a=sml(1):
             e.activation(out=o, in_=i, func=AF.Square, accum_out=a), inc=s_ac,
             reads=(nid,), writes=(xid, "sm1"))
        p.wait("dve", s_ac)
        p.dve(lambda e, a=sml(1), o=sml(2):
              e.tensor_scalar(out=o, in0=a, scalar1=1.0 / D, scalar2=EPS, op0=ALU.mult, op1=ALU.add),
              reads=("sm1",), writes=("sm2",))
        p.dve(lambda e, i=sml(2), o=sml(3): e.reciprocal(out=o, in_=i),
              reads=("sm2",), writes=("sm3",), inc=s_dv)
        p.wait("act", s_dv)
        p.op("act", lambda e, i=sml(3), o=sml(2): e.activation(out=o, in_=i, func=AF.Sqrt), inc=s_ac,
             reads=("sm3",), writes=("sm2",))
        p.wait("dve", s_ac)
        p.dve(lambda e, i=xnsl, m=sml(2):
              e.tensor_scalar(out=i, in0=i, scalar1=m, scalar2=None, op0=ALU.mult),
              reads=(nid, "sm2"), writes=(nid,), inc=s_dv)
        p.wait("pe", s_dv)
        for q in range(4):
            bank = ps[q]
            for j in range(4):
                k = q * 4 + j
                p.op("pe", lambda e, i=xnsl[:, k * 128:(k + 1) * 128], o=bank[:, j * 128:(j + 1) * 128]:
                     e.transpose(out=o, in_=i, identity=ident32[:]))
            p.milestone("pe", s_pe)
            p.wait("act", s_pe)
            dst = xnT[:, :].rearrange("p (k t) -> p k t", k=NK)[:, q * 4:(q + 1) * 4, tj * 128:(tj + 1) * 128]
            srcv = bank[:, 0:512].rearrange("p (k t) -> p k t", k=4)
            p.op("act", lambda e, o=dst, i=srcv: e.copy(out=o, in_=i), inc=s_ac, writes=("xnT",))
        ln_done_pe.append(s_pe.n)
        p.wait("pe", s_ac)
    tap("xnT", xnT[:], (128, NK * 1024), F16)

    poolA = Pool(WPA, 16, 384, semA)
    poolB = Pool(WPB, 16, 384, semB)
    poolC = Pool(WPC, 8, 256, semC)

    for g in range(8):
        banks = [ps[(g % 2) * 4 + i] for i in range(4)]
        lds = [pool_load(poolA, k, W_in[k * 128:(k + 1) * 128, g * 256:(g + 1) * 256], 256)
               for k in range(NK)]
        if g >= 2:
            p.wait("pe", s_ac)
        for k in range(NK):
            sl = poolA.slot(k)
            p.wait("pe", lds[k][0], lds[k][1])
            for j in range(2):
                for tt in range(NTT):
                    p.op("pe", lambda e, o=banks[j * 2 + tt][:], w=sl[:, j * 128:(j + 1) * 128],
                         r=xnT_k(k, tt), st=(k == 0), fin=(k == NK - 1):
                         e.matmul(out=o, lhsT=w, rhs=r, start=st, stop=fin))
        p.milestone("pe", s_pe)
        for k in range(NK):
            poolA.guard[k] = s_pe.n
        p.wait("act", s_pe)
        for j in range(2):
            oc = g * 2 + j
            for tt in range(NTT):
                p.op("act", lambda e, o=XH_k(oc, tt), i=banks[j * 2 + tt][:], bb=bcol_in[:, oc:oc + 1]:
                     e.activation(out=o, in_=i, func=AF.Identity, bias=bb), inc=s_ac,
                     reads=("bcol_in",), writes=("xh",))
    xh_done_ac = s_ac.n
    tap("x_h", XH[:], (128, NK * 1024), F16)

    p.wait("pe", s_ac, xh_done_ac)
    for tj in range(8):
        bank = ps[tj % 2]
        if tj >= 2:
            p.wait("pe", s_dv)
        for k in range(NK):
            p.op("pe", lambda e, o=bank[:, 0:8],
                 w=xnT[:, k * 1024 + tj * 128:k * 1024 + (tj + 1) * 128],
                 r=rwT[:, k * 8:(k + 1) * 8], st=(k == 0), fin=(k == NK - 1):
                 e.matmul(out=o, lhsT=w, rhs=r, start=st, stop=fin))
        p.milestone("pe", s_pe)
        p.wait("dve", s_pe)
        Lt = scr(6, 8)
        V8 = scr(6, 1024)[:, 16:24]
        C1 = scr(6, 1024)[:, 32:40]
        C2 = scr(6, 1024)[:, 40:48]
        p.dve(lambda e, o=Lt, i=bank[:, 0:8]: e.tensor_copy(out=o, in_=i), writes=("rlog",))
        p.dve(lambda e, o=V8, i=Lt: e.max(out=o, in_=i), reads=("rlog",), writes=("rmax",))
        p.dve(lambda e, o=sml(8), a=V8[:, 0:1], bb=V8[:, 1:2]: e.tensor_sub(out=o, in0=a, in1=bb),
              reads=("rmax",), writes=("sm8",), inc=s_dv)
        p.wait("act", s_dv)
        p.op("act", lambda e, o=sml(9), i=sml(8): e.activation(out=o, in_=i, func=AF.Sigmoid), inc=s_ac,
             reads=("sm8",), writes=("sm9",))
        p.wait("dve", s_ac)
        p.dve(lambda e, o=sml(10), i=sml(9):
              e.tensor_scalar(out=o, in0=i, scalar1=-1.0, scalar2=1.0, op0=ALU.mult, op1=ALU.add),
              reads=("sm9",), writes=("sm10",))
        p.dve(lambda e, o=C1, i=Lt, v=V8[:, 0:1], w=sml(9):
              e.tensor_scalar(out=o, in0=i, scalar1=v, scalar2=w, op0=ALU.is_equal, op1=ALU.mult),
              reads=("rlog", "rmax", "sm9"), writes=("rc1",))
        p.dve(lambda e, o=C2, i=Lt, v=V8[:, 1:2], w=sml(10):
              e.tensor_scalar(out=o, in0=i, scalar1=v, scalar2=w, op0=ALU.is_equal, op1=ALU.mult),
              reads=("rlog", "rmax", "sm10"), writes=("rc2",))
        p.dve(lambda e, o=C1, a=C1, bb=C2: e.tensor_add(out=o, in0=a, in1=bb),
              reads=("rc1", "rc2"), writes=("rc1",), inc=s_dv)
        p.wait("pe", s_dv)
        bank2 = ps[2 + tj % 2]
        p.op("pe", lambda e, o=bank2[0:8, 0:128], i=C1:
             e.transpose(out=o, in_=i, identity=ident32[:]), inc=s_pe)
        p.wait("dve", s_pe)
        p.dve(lambda e, o=Crows[0:8, tj * 128:(tj + 1) * 128], i=bank2[0:8, 0:128]:
              e.tensor_copy(out=o, in_=i), writes=("crows",), inc=s_dv)
    tap("Crows", Crows[:], (8, T))

    for tt in range(NTT):
        bank = ps[4]
        if tt > 0:
            p.wait("pe", s_dv)
        for k in range(NK):
            p.op("pe", lambda e, o=bank[0:1, :], w=sgw_col[:, k:k + 1], r=xnT_k(k, tt),
                 st=(k == 0), fin=(k == NK - 1):
                 e.matmul(out=o, lhsT=w, rhs=r, start=st, stop=fin))
        p.milestone("pe", s_pe)
        p.wait("act", s_pe)
        p.op("act", lambda e, o=grow[0:1, tt * TT:(tt + 1) * TT], i=bank[0:1, :]:
             e.activation(out=o, in_=i, func=AF.Sigmoid), inc=s_ac, writes=("grow",))
        p.wait("pe", s_ac)
        bank5 = ps[5]
        p.op("pe", lambda e, o=bank5[:], r=grow[0:1, tt * TT:(tt + 1) * TT]:
             e.matmul(out=o, lhsT=ones_t[0:1, 0:128], rhs=r, start=True, stop=True), inc=s_pe)
        p.wait("dve", s_pe)
        p.dve(lambda e, o=gate_b[:, tt * TT:(tt + 1) * TT], i=bank5[:]: e.tensor_copy(out=o, in_=i),
              writes=("gate_b",), inc=s_dv)
    tap("gate_b", gate_b[:], (128, T))

    GX, GA, LA, THm, ASQ, M1, M2 = (scr(i) for i in range(7))
    MAGt = GA
    PRE = ASQ
    TH = GX

    p.wait("pe", s_wg, wg_wd)
    gate_evac_ac = [0, 0]
    for b in range(NB):
        boff = (b % 2) * 4
        banks = [ps[boff + i] for i in range(4)]
        if b < 2:
            p.wait("pe", s_dv)
        else:
            p.wait("pe", s_ac, gate_evac_ac[b % 2])
        for g in range(2):
            for tt in range(NTT):
                for kk in range(2):
                    p.op("pe", lambda e, o=banks[g * 2 + tt][:], w=wg_tile(g, b, kk),
                         r=XH_k(2 * b + kk, tt), st=(kk == 0), fin=(kk == 1):
                         e.matmul(out=o, lhsT=w, rhs=r, start=st, stop=fin))
        p.milestone("pe", s_pe)
        p.wait("act", s_pe)
        p.wait("act", s_dv)
        for g, dstt, bc, gnm, bnm in ((0, GX, bcol_gx, "gx", "bcol_gx"), (1, GA, bcol_ga, "ga", "bcol_ga")):
            for tt in range(NTT):
                p.op("act", lambda e, o=dstt[:, tt * TT:(tt + 1) * TT], i=banks[g * 2 + tt][:],
                     bb=bc[:, b:b + 1]:
                     e.activation(out=o, in_=i, func=AF.Sigmoid, bias=bb), inc=s_ac,
                     reads=(bnm,), writes=(gnm,))
        gate_evac_ac[b % 2] = s_ac.n
        p.wait("dve", s_ac)
        p.dve(lambda e, o=LA, i=GA, s=sp_col[:, b:b + 1]:
              e.tensor_scalar(out=o, in0=i, scalar1=s, scalar2=-8.0, op0=ALU.mult, op1=ALU.mult),
              reads=("ga",), writes=("la",))
        p.dve(lambda e, o=THm, i=GA, s=ai_col[:, b:b + 1]:
              e.tensor_scalar(out=o, in0=i, scalar1=s, scalar2=None, op0=ALU.mult),
              reads=("ga",), writes=("th",), inc=s_dv)
        p.wait("act", s_dv)
        p.op("act", lambda e, o=MAGt, i=LA: e.activation(out=o, in_=i, func=AF.Exp),
             reads=("la",), writes=("ga",))
        p.op("act", lambda e, o=ASQ, i=LA: e.activation(out=o, in_=i, func=AF.Exp, scale=2.0), inc=s_ac,
             reads=("la",), writes=("asq",))
        p.dve(lambda e, o=sml(4), i=LA: e.reduce_sum(out=o, in_=i, axis=AXX),
              reads=("la",), writes=("sm4",))
        p.wait("dve", s_ac)
        p.dve(lambda e, o=M1, i=ASQ:
              e.tensor_scalar(out=o, in0=i, scalar1=-1.0, scalar2=1.0, op0=ALU.mult, op1=ALU.add),
              reads=("asq",), writes=("m1",), inc=s_dv)
        p.wait("act", s_dv)
        p.op("act", lambda e, o=M2, i=M1: e.activation(out=o, in_=i, func=AF.Sqrt), inc=s_ac,
             reads=("m1",), writes=("m2",))
        p.wait("dve", s_ac)
        p.dve(lambda e, o=PRE, a=GX, bb=M2: e.tensor_mul(out=o, in0=a, in1=bb),
              reads=("gx", "m2"), writes=("asq",))
        for tt in range(NTT):
            p.dve(lambda e, o=TH[:, tt * TT:(tt + 1) * TT], d0=THm[:, tt * TT:(tt + 1) * TT], d1=zeros_h[:]:
                  e.tensor_tensor_scan(out=o, data0=d0, data1=d1, initial=0.0, op0=ALU.add, op1=ALU.add),
                  reads=("th", "zeros_h"), writes=("gx",))
        p.dve(lambda e, o=sml(5), a=TH[:, TT - 1:TT], bb=TH[:, T - 1:T]: e.tensor_add(out=o, in0=a, in1=bb),
              reads=("gx",), writes=("sm5",))
        p.dve(lambda e, o=M2, i=TH:
              e.tensor_scalar(out=o, in0=i, scalar1=1.0 / (2 * PI), scalar2=MAGIC, op0=ALU.mult, op1=ALU.add),
              reads=("gx",), writes=("m2",))
        p.dve(lambda e, o=M2, i=M2: e.tensor_scalar(out=o, in0=i, scalar1=-MAGIC, scalar2=None, op0=ALU.add),
              reads=("m2",), writes=("m2",))
        p.dve(lambda e, o=M1, i=M2, t=TH:
              e.scalar_tensor_tensor(out=o, in0=i, scalar=-2 * PI, in1=t, op0=ALU.mult, op1=ALU.add),
              reads=("m2", "gx"), writes=("m1",))
        p.dve(lambda e, o=M1, i=M1:
              e.tensor_scalar(out=o, in0=i, scalar1=-PI, scalar2=PI, op0=ALU.max, op1=ALU.min),
              reads=("m1",), writes=("m1",), inc=s_dv)
        p.wait("act", s_dv)
        p.op("act", lambda e, o=snb[:], i=M1: e.activation(out=o, in_=i, func=AF.Sin), inc=s_ac,
             reads=("m1",), writes=("sn",))
        p.dve(lambda e, o=M2, i=M1:
              e.tensor_scalar(out=o, in0=i, scalar1=PI / 2, scalar2=-2 * PI, op0=ALU.is_gt, op1=ALU.mult),
              reads=("m1",), writes=("m2",))
        p.dve(lambda e, o=M2, i=M2, r=M1:
              e.scalar_tensor_tensor(out=o, in0=i, scalar=PI / 2, in1=r, op0=ALU.add, op1=ALU.add),
              reads=("m1", "m2"), writes=("m2",))
        p.dve(lambda e, o=M2, i=M2:
              e.tensor_scalar(out=o, in0=i, scalar1=-PI, scalar2=PI, op0=ALU.max, op1=ALU.min),
              reads=("m2",), writes=("m2",), inc=s_dv)
        p.wait("act", s_dv)
        p.op("act", lambda e, o=csb[:], i=M2: e.activation(out=o, in_=i, func=AF.Sin), inc=s_ac,
             reads=("m2",), writes=("cs",))
        p.wait("dve", s_ac)
        XR = XH_k(b)
        XI = XH_k(8 + b)
        p.dve(lambda e, o=M1, a=csb[:], bb=XR: e.tensor_mul(out=o, in0=a, in1=bb),
              reads=("cs",), writes=("m1",))
        p.dve(lambda e, o=M2, a=snb[:], bb=XI: e.tensor_mul(out=o, in0=a, in1=bb),
              reads=("sn",), writes=("m2",))
        p.dve(lambda e, o=M1, a=M1, bb=M2: e.tensor_add(out=o, in0=a, in1=bb),
              reads=("m1", "m2"), writes=("m1",))
        p.dve(lambda e, o=xtr[:], a=M1, bb=PRE: e.tensor_mul(out=o, in0=a, in1=bb),
              reads=("m1", "asq"), writes=("xtr",))
        p.dve(lambda e, o=M1, a=csb[:], bb=XI: e.tensor_mul(out=o, in0=a, in1=bb),
              reads=("cs",), writes=("m1",))
        p.dve(lambda e, o=M2, a=snb[:], bb=XR: e.tensor_mul(out=o, in0=a, in1=bb),
              reads=("sn",), writes=("m2",))
        p.dve(lambda e, o=M1, a=M1, bb=M2: e.tensor_sub(out=o, in0=a, in1=bb),
              reads=("m1", "m2"), writes=("m1",))
        p.dve(lambda e, o=xti[:], a=M1, bb=PRE: e.tensor_mul(out=o, in0=a, in1=bb),
              reads=("m1", "asq"), writes=("xti",))
        p.dve(lambda e, o=grL[:, 0:TT], d0=MAGt[:, 0:TT], d1=xtr[:, 0:TT]:
              e.tensor_tensor_scan(out=o, data0=d0, data1=d1, initial=0.0, op0=ALU.mult, op1=ALU.add),
              reads=("ga", "xtr"), writes=("grL",))
        p.dve(lambda e, o=giL[:, 0:TT], d0=MAGt[:, 0:TT], d1=xti[:, 0:TT]:
              e.tensor_tensor_scan(out=o, data0=d0, data1=d1, initial=0.0, op0=ALU.mult, op1=ALU.add),
              reads=("ga", "xti"), writes=("giL",))
        ce, se = csb[:, TT - 1:TT], snb[:, TT - 1:TT]
        gre, gie = grL[:, TT - 1:TT], giL[:, TT - 1:TT]
        p.dve(lambda e, o=sml(6), a=ce, bb=gre: e.tensor_mul(out=o, in0=a, in1=bb),
              reads=("cs", "grL"), writes=("sm6",))
        p.dve(lambda e, o=sml(11), a=se, bb=gie: e.tensor_mul(out=o, in0=a, in1=bb),
              reads=("sn", "giL"), writes=("sm11",))
        p.dve(lambda e, o=sml(6), a=sml(6), bb=sml(11): e.tensor_sub(out=o, in0=a, in1=bb),
              reads=("sm6", "sm11"), writes=("sm6",))
        p.dve(lambda e, o=sml(7), a=se, bb=gre: e.tensor_mul(out=o, in0=a, in1=bb),
              reads=("sn", "grL"), writes=("sm7",))
        p.dve(lambda e, o=sml(11), a=ce, bb=gie: e.tensor_mul(out=o, in0=a, in1=bb),
              reads=("cs", "giL"), writes=("sm11",))
        p.dve(lambda e, o=sml(7), a=sml(7), bb=sml(11): e.tensor_add(out=o, in0=a, in1=bb),
              reads=("sm7", "sm11"), writes=("sm7",))
        p.dve(lambda e, o=grL[:, TT:T], d0=MAGt[:, TT:T], d1=xtr[:, TT:T], ini=sml(6):
              e.tensor_tensor_scan(out=o, data0=d0, data1=d1, initial=ini, op0=ALU.mult, op1=ALU.add),
              reads=("ga", "xtr", "sm6"), writes=("grL",))
        p.dve(lambda e, o=giL[:, TT:T], d0=MAGt[:, TT:T], d1=xti[:, TT:T], ini=sml(7):
              e.tensor_tensor_scan(out=o, data0=d0, data1=d1, initial=ini, op0=ALU.mult, op1=ALU.add),
              reads=("ga", "xti", "sm7"), writes=("giL",))
        p.dve(lambda e, o=sml(8), i=sml(5):
              e.tensor_scalar(out=o, in0=i, scalar1=1.0 / (2 * PI), scalar2=MAGIC, op0=ALU.mult, op1=ALU.add),
              reads=("sm5",), writes=("sm8",))
        p.dve(lambda e, o=sml(8), i=sml(8): e.tensor_scalar(out=o, in0=i, scalar1=-MAGIC, scalar2=None, op0=ALU.add),
              reads=("sm8",), writes=("sm8",))
        p.dve(lambda e, o=sml(9), i=sml(8), t=sml(5):
              e.scalar_tensor_tensor(out=o, in0=i, scalar=-2 * PI, in1=t, op0=ALU.mult, op1=ALU.add),
              reads=("sm8", "sm5"), writes=("sm9",))
        p.dve(lambda e, o=sml(9), i=sml(9):
              e.tensor_scalar(out=o, in0=i, scalar1=-PI, scalar2=PI, op0=ALU.max, op1=ALU.min),
              reads=("sm9",), writes=("sm9",))
        p.dve(lambda e, o=sml(10), i=sml(9):
              e.tensor_scalar(out=o, in0=i, scalar1=PI / 2, scalar2=-2 * PI, op0=ALU.is_gt, op1=ALU.mult),
              reads=("sm9",), writes=("sm10",))
        p.dve(lambda e, o=sml(10), i=sml(10), r=sml(9):
              e.scalar_tensor_tensor(out=o, in0=i, scalar=PI / 2, in1=r, op0=ALU.add, op1=ALU.add),
              reads=("sm9", "sm10"), writes=("sm10",))
        p.dve(lambda e, o=sml(10), i=sml(10):
              e.tensor_scalar(out=o, in0=i, scalar1=-PI, scalar2=PI, op0=ALU.max, op1=ALU.min),
              reads=("sm10",), writes=("sm10",), inc=s_dv)
        p.wait("act", s_dv)
        p.op("act", lambda e, o=sml(12), i=sml(9): e.activation(out=o, in_=i, func=AF.Sin),
             reads=("sm9",), writes=("sm12",))
        p.op("act", lambda e, o=sml(13), i=sml(10): e.activation(out=o, in_=i, func=AF.Sin),
             reads=("sm10",), writes=("sm13",))
        p.op("act", lambda e, o=sml(14), i=sml(4): e.activation(out=o, in_=i, func=AF.Exp), inc=s_ac,
             reads=("sm4",), writes=("sm14",))
        p.wait("dve", s_ac)
        p.dve(lambda e, o=S_t[:, 0:1], a=sml(14), bb=sml(13): e.tensor_mul(out=o, in0=a, in1=bb),
              reads=("sm14", "sm13"), writes=("S_t",))
        p.dve(lambda e, o=S_t[:, 1:2], a=sml(14), bb=sml(12): e.tensor_mul(out=o, in0=a, in1=bb),
              reads=("sm14", "sm12"), writes=("S_t",))
        ce1, se1 = csb[:, T - 1:T], snb[:, T - 1:T]
        gre1, gie1 = grL[:, T - 1:T], giL[:, T - 1:T]
        p.dve(lambda e, o=sml(11), a=ce1, bb=gre1: e.tensor_mul(out=o, in0=a, in1=bb),
              reads=("cs", "grL"), writes=("sm11",))
        p.dve(lambda e, o=sml(12), a=se1, bb=gie1: e.tensor_mul(out=o, in0=a, in1=bb),
              reads=("sn", "giL"), writes=("sm12",))
        p.dve(lambda e, o=S_t[:, 2:3], a=sml(11), bb=sml(12): e.tensor_sub(out=o, in0=a, in1=bb),
              reads=("sm11", "sm12"), writes=("S_t",))
        p.dve(lambda e, o=sml(11), a=se1, bb=gre1: e.tensor_mul(out=o, in0=a, in1=bb),
              reads=("sn", "grL"), writes=("sm11",))
        p.dve(lambda e, o=sml(12), a=ce1, bb=gie1: e.tensor_mul(out=o, in0=a, in1=bb),
              reads=("cs", "giL"), writes=("sm12",))
        p.dve(lambda e, o=S_t[:, 3:4], a=sml(11), bb=sml(12): e.tensor_add(out=o, in0=a, in1=bb),
              reads=("sm11", "sm12"), writes=("S_t",), inc=s_dv)
        p.wait("sp", s_dv)
        p.op("sp", lambda e, o=cc_in[b], i=S_t[:]: e.dma_start(out=o, in_=i), inc=s_ex, inck=16)
        p.wait("gp", s_ex)
        p.op("gp", lambda e, i=cc_in[b], o=cc_out[b]: e.collective_compute(
            "AllGather", ALU.bypass, replica_groups=[list(range(NCORES))], ins=[i], outs=[o]),
            inc=s_cc)
        p.wait("sp", s_cc)
        for r in range(NCORES):
            p.op("sp", lambda e, o=G_t[:, r * 4:(r + 1) * 4], i=cc_out[b][r]: e.dma_start(out=o, in_=i),
                 inc=s_ex, inck=16)
        p.wait("dve", s_ex)
        for r in range(NCORES):
            Ar = G_t[:, r * 4 + 0:r * 4 + 1]
            Ai = G_t[:, r * 4 + 1:r * 4 + 2]
            Xr = G_t[:, r * 4 + 2:r * 4 + 3]
            Xi = G_t[:, r * 4 + 3:r * 4 + 4]
            if r == 0:
                Hpr, Hpi = zeros_h[:, 0:1], zeros_h[:, 1:2]
            else:
                Hpr, Hpi = Hall[:, r - 1:r], Hall[:, 8 + r - 1:8 + r]
            p.dve(lambda e, o=sml(8), a=Ar, bb=Hpr: e.tensor_mul(out=o, in0=a, in1=bb),
                  reads=("G", "Hall"), writes=("sm8",))
            p.dve(lambda e, o=sml(9), a=Ai, bb=Hpi: e.tensor_mul(out=o, in0=a, in1=bb),
                  reads=("G", "Hall"), writes=("sm9",))
            p.dve(lambda e, o=sml(8), a=sml(8), bb=sml(9): e.tensor_sub(out=o, in0=a, in1=bb),
                  reads=("sm8", "sm9"), writes=("sm8",))
            p.dve(lambda e, o=Hall[:, r:r + 1], a=sml(8), bb=Xr: e.tensor_add(out=o, in0=a, in1=bb),
                  reads=("sm8", "G"), writes=("Hall",))
            p.dve(lambda e, o=sml(8), a=Ar, bb=Hpi: e.tensor_mul(out=o, in0=a, in1=bb),
                  reads=("G", "Hall"), writes=("sm8",))
            p.dve(lambda e, o=sml(9), a=Ai, bb=Hpr: e.tensor_mul(out=o, in0=a, in1=bb),
                  reads=("G", "Hall"), writes=("sm9",))
            p.dve(lambda e, o=sml(8), a=sml(8), bb=sml(9): e.tensor_add(out=o, in0=a, in1=bb),
                  reads=("sm8", "sm9"), writes=("sm8",))
            p.dve(lambda e, o=Hall[:, 8 + r:8 + r + 1], a=sml(8), bb=Xi: e.tensor_add(out=o, in0=a, in1=bb),
                  reads=("sm8", "G"), writes=("Hall",))
        p.dve(lambda e, o=G_t[:, 0:8], a=Hall[:, 0:8], bb=oh8b[:]: e.tensor_mul(out=o, in0=a, in1=bb),
              reads=("Hall", "oh8b"), writes=("G",))
        p.dve(lambda e, o=sml(6), i=G_t[:, 0:8]: e.reduce_sum(out=o, in_=i, axis=AXX),
              reads=("G",), writes=("sm6",))
        p.dve(lambda e, o=G_t[:, 0:8], a=Hall[:, 8:16], bb=oh8b[:]: e.tensor_mul(out=o, in0=a, in1=bb),
              reads=("Hall", "oh8b"), writes=("G",))
        p.dve(lambda e, o=sml(7), i=G_t[:, 0:8]: e.reduce_sum(out=o, in_=i, axis=AXX),
              reads=("G",), writes=("sm7",))
        p.dve(lambda e, o=gpr[:, 0:TT], d0=MAGt[:, 0:TT], d1=zeros_h[:], ini=sml(6):
              e.tensor_tensor_scan(out=o, data0=d0, data1=d1, initial=ini, op0=ALU.mult, op1=ALU.add),
              reads=("ga", "zeros_h", "sm6"), writes=("gpr",))
        p.dve(lambda e, o=gpi[:, 0:TT], d0=MAGt[:, 0:TT], d1=zeros_h[:], ini=sml(7):
              e.tensor_tensor_scan(out=o, data0=d0, data1=d1, initial=ini, op0=ALU.mult, op1=ALU.add),
              reads=("ga", "zeros_h", "sm7"), writes=("gpi",))
        gre2, gie2 = gpr[:, TT - 1:TT], gpi[:, TT - 1:TT]
        p.dve(lambda e, o=sml(6), a=ce, bb=gre2: e.tensor_mul(out=o, in0=a, in1=bb),
              reads=("cs", "gpr"), writes=("sm6",))
        p.dve(lambda e, o=sml(11), a=se, bb=gie2: e.tensor_mul(out=o, in0=a, in1=bb),
              reads=("sn", "gpi"), writes=("sm11",))
        p.dve(lambda e, o=sml(6), a=sml(6), bb=sml(11): e.tensor_sub(out=o, in0=a, in1=bb),
              reads=("sm6", "sm11"), writes=("sm6",))
        p.dve(lambda e, o=sml(7), a=se, bb=gre2: e.tensor_mul(out=o, in0=a, in1=bb),
              reads=("sn", "gpr"), writes=("sm7",))
        p.dve(lambda e, o=sml(11), a=ce, bb=gie2: e.tensor_mul(out=o, in0=a, in1=bb),
              reads=("cs", "gpi"), writes=("sm11",))
        p.dve(lambda e, o=sml(7), a=sml(7), bb=sml(11): e.tensor_add(out=o, in0=a, in1=bb),
              reads=("sm7", "sm11"), writes=("sm7",))
        p.dve(lambda e, o=gpr[:, TT:T], d0=MAGt[:, TT:T], d1=zeros_h[:], ini=sml(6):
              e.tensor_tensor_scan(out=o, data0=d0, data1=d1, initial=ini, op0=ALU.mult, op1=ALU.add),
              reads=("ga", "zeros_h", "sm6"), writes=("gpr",))
        p.dve(lambda e, o=gpi[:, TT:T], d0=MAGt[:, TT:T], d1=zeros_h[:], ini=sml(7):
              e.tensor_tensor_scan(out=o, data0=d0, data1=d1, initial=ini, op0=ALU.mult, op1=ALU.add),
              reads=("ga", "zeros_h", "sm7"), writes=("gpi",))
        p.dve(lambda e, o=M1, a=grL[:], bb=gpr[:]: e.tensor_add(out=o, in0=a, in1=bb),
              reads=("grL", "gpr"), writes=("m1",))
        p.dve(lambda e, o=M2, a=giL[:], bb=gpi[:]: e.tensor_add(out=o, in0=a, in1=bb),
              reads=("giL", "gpi"), writes=("m2",))
        p.dve(lambda e, o=THm, a=csb[:], bb=M1: e.tensor_mul(out=o, in0=a, in1=bb),
              reads=("cs", "m1"), writes=("th",))
        p.dve(lambda e, o=PRE, a=snb[:], bb=M2: e.tensor_mul(out=o, in0=a, in1=bb),
              reads=("sn", "m2"), writes=("asq",))
        p.dve(lambda e, o=h_k(b), a=THm, bb=PRE: e.tensor_sub(out=o, in0=a, in1=bb),
              reads=("th", "asq"), writes=("h",))
        p.dve(lambda e, o=THm, a=snb[:], bb=M1: e.tensor_mul(out=o, in0=a, in1=bb),
              reads=("sn", "m1"), writes=("th",))
        p.dve(lambda e, o=PRE, a=csb[:], bb=M2: e.tensor_mul(out=o, in0=a, in1=bb),
              reads=("cs", "m2"), writes=("asq",))
        p.dve(lambda e, o=h_k(8 + b), a=THm, bb=PRE: e.tensor_add(out=o, in0=a, in1=bb),
              reads=("th", "asq"), writes=("h",), inc=s_dv)
    h_ready_dv = s_dv.n
    tap("h", HREG[:, 0:NK * 1024], (128, NK * 1024), F16)

    p.wait("pe", s_dv, h_ready_dv)
    p.wait("pe", s_ac)
    for g in range(8):
        banks = [ps[(g % 2) * 4 + i] for i in range(4)]
        lds = [pool_load(poolA, k, W_out[k * 128:(k + 1) * 128, g * 256:(g + 1) * 256], 256)
               for k in range(NK)]
        if g >= 2:
            p.wait("pe", s_ac)
        for k in range(NK):
            sl = poolA.slot(k)
            p.wait("pe", lds[k][0], lds[k][1])
            for j in range(2):
                for tt in range(NTT):
                    p.op("pe", lambda e, o=banks[j * 2 + tt][:], w=sl[:, j * 128:(j + 1) * 128],
                         r=h_k(k, tt), st=(k == 0), fin=(k == NK - 1):
                         e.matmul(out=o, lhsT=w, rhs=r, start=st, stop=fin))
        p.milestone("pe", s_pe)
        for k in range(NK):
            poolA.guard[k] = s_pe.n
        p.wait("act", s_pe)
        for j in range(2):
            oc = g * 2 + j
            for tt in range(NTT):
                p.op("act", lambda e, o=ACC_k(oc, tt), i=banks[j * 2 + tt][:], bb=bcol_out[:, oc:oc + 1]:
                     e.activation(out=o, in_=i, func=AF.Identity, bias=bb), inc=s_ac,
                     reads=("bcol_out",), writes=("acc",))
    tap("y_attn", XH[:], (128, NK * 1024), F16)

    for half in range(2):
        prod_dv = 0
        for og in range(11):
            oc0 = half * 22 + og * 2
            gb = [ps[i] for i in range(4)]
            lds = [pool_load(poolA, k, sw_gate[k * 128:(k + 1) * 128, oc0 * 128:(oc0 + 2) * 128], 256)
                   for k in range(NK)]
            if og == 0:
                p.wait("pe", s_dv)
            p.wait("pe", s_ac)
            for k in range(NK):
                sl = poolA.slot(k)
                p.wait("pe", lds[k][0], lds[k][1])
                for j in range(2):
                    for tt in range(NTT):
                        p.op("pe", lambda e, o=gb[j * 2 + tt][:], w=sl[:, j * 128:(j + 1) * 128],
                             r=xnT_k(k, tt), st=(k == 0), fin=(k == NK - 1):
                             e.matmul(out=o, lhsT=w, rhs=r, start=st, stop=fin))
                p.milestone("pe", s_pe)
                poolA.guard[k] = s_pe.n
            gate_pe = s_pe.n
            ub = [ps[4 + i] for i in range(4)]
            lds = [pool_load(poolB, k, sw_up[k * 128:(k + 1) * 128, oc0 * 128:(oc0 + 2) * 128], 256)
                   for k in range(NK)]
            if og > 0:
                p.wait("pe", s_dv, prod_dv)
            for k in range(NK):
                sl = poolB.slot(k)
                p.wait("pe", lds[k][0], lds[k][1])
                for j in range(2):
                    for tt in range(NTT):
                        p.op("pe", lambda e, o=ub[j * 2 + tt][:], w=sl[:, j * 128:(j + 1) * 128],
                             r=xnT_k(k, tt), st=(k == 0), fin=(k == NK - 1):
                             e.matmul(out=o, lhsT=w, rhs=r, start=st, stop=fin))
                p.milestone("pe", s_pe)
                poolB.guard[k] = s_pe.n
            up_pe = s_pe.n
            p.wait("act", s_pe, gate_pe)
            if og > 0:
                p.wait("act", s_dv, prod_dv)
            gs_sl = {}
            for j in range(2):
                for tt in range(NTT):
                    base = j * NTT + tt
                    slc = scr(base // 2, 1024)[:, (base % 2) * TT:(base % 2) * TT + TT]
                    gs_sl[(j, tt)] = slc
                    p.op("act", lambda e, o=slc, i=gb[j * 2 + tt][:]:
                         e.activation(out=o, in_=i, func=AF.Silu), inc=s_ac, writes=("gs",))
            p.wait("dve", s_ac)
            p.wait("dve", s_pe, up_pe)
            for j in range(2):
                li = og * 2 + j
                for tt in range(NTT):
                    tmp = scr(3, 1024)[:, (tt) * TT:(tt) * TT + TT]
                    p.dve(lambda e, o=tmp, a=ub[j * 2 + tt][:], bb=gate_b[:, tt * TT:(tt + 1) * TT]:
                          e.tensor_mul(out=o, in0=a, in1=bb), reads=("gate_b",), writes=("tmp",))
                    p.dve(lambda e, o=sh_k(li, tt), a=gs_sl[(j, tt)], bb=tmp:
                          e.tensor_mul(out=o, in0=a, in1=bb), reads=("tmp", "gs"), writes=("sh",))
            p.milestone("dve", s_dv)
            prod_dv = s_dv.n
        sh_dv = s_dv.n
        acc_dv = {}
        for ocp in range(8):
            banks = [ps[(ocp % 2) * 4 + i] for i in range(4)]
            wd_ms = []

            def issue(i, ocp=ocp, half=half, wd_ms=wd_ms):
                wd_ms.append(pool_load(
                    poolC, i, sw_down[(half * 22 + i) * 128:(half * 22 + i + 1) * 128,
                                      ocp * 256:(ocp + 1) * 256]))

            for i in range(4):
                issue(i)
            for i in range(22):
                p.wait("pe", wd_ms[i][0], wd_ms[i][1])
                if i == 0:
                    p.wait("pe", s_dv, sh_dv)
                    if ocp == 0:
                        p.wait("pe", s_ac)
                    elif ocp >= 2:
                        p.wait("pe", s_dv, acc_dv[ocp - 2])
                sl = poolC.slot(i)
                for j in range(2):
                    for tt in range(NTT):
                        p.op("pe", lambda e, o=banks[j * 2 + tt][:], w=sl[:, j * 128:(j + 1) * 128],
                             r=sh_k(i, tt), st=(i == 0), fin=(i == 21):
                             e.matmul(out=o, lhsT=w, rhs=r, start=st, stop=fin))
                p.milestone("pe", s_pe)
                poolC.guard[i % 8] = s_pe.n
                if i + 4 < 22:
                    issue(i + 4)
            p.wait("dve", s_pe)
            for j in range(2):
                oc = ocp * 2 + j
                for tt in range(NTT):
                    p.dve(lambda e, o=ACC_k(oc, tt), a=ACC_k(oc, tt), bb=banks[j * 2 + tt][:]:
                          e.tensor_add(out=o, in0=a, in1=bb), reads=("acc",), writes=("acc",))
            p.milestone("dve", s_dv)
            acc_dv[ocp] = s_dv.n
    tap("acc_shared", XH[:], (128, NK * 1024), F16)

    igroups = [(0, 2), (2, 2), (4, 2), (6, 2), (8, 2), (10, 1)]
    for ex in range(E):
        par = ex % 2
        CBC = scr(4)
        for tt in range(NTT):
            bank = ps[6]
            p.wait("pe", s_dv)
            p.op("pe", lambda e, o=bank[:], r=Crows[0:8, tt * TT:(tt + 1) * TT], x_=ex:
                 e.matmul(out=o, lhsT=e8sel[0:8, x_ * 128:(x_ + 1) * 128], rhs=r, start=True, stop=True), inc=s_pe)
            p.wait("dve", s_pe)
            p.dve(lambda e, o=CBC[:, tt * TT:(tt + 1) * TT], i=bank[:]: e.tensor_copy(out=o, in_=i),
                  writes=("cbc",), inc=s_dv)
        prod_dv = s_dv.n
        for gi, (i0, ni) in enumerate(igroups):
            gbanks = [ps[i] for i in range(2 * ni)]
            ubanks = [ps[4 + i] for i in range(2 * ni)]
            ldg = [pool_load(poolA, k, w_gate[ex, k * 128:(k + 1) * 128, i0 * 128:(i0 + ni) * 128], ni * 128)
                   for k in range(NK)]
            ldu = [pool_load(poolB, k, w_up[ex, k * 128:(k + 1) * 128, i0 * 128:(i0 + ni) * 128], ni * 128)
                   for k in range(NK)]
            p.wait("pe", s_ac)
            if gi == 0:
                p.wait("pe", s_dv)
            for k in range(NK):
                slg = poolA.slot(k)
                p.wait("pe", ldg[k][0], ldg[k][1])
                for j in range(ni):
                    for tt in range(NTT):
                        p.op("pe", lambda e, o=gbanks[j * 2 + tt][:], w=slg[:, j * 128:(j + 1) * 128],
                             r=xnT_k(k, tt), st=(k == 0), fin=(k == NK - 1):
                             e.matmul(out=o, lhsT=w, rhs=r, start=st, stop=fin))
                p.milestone("pe", s_pe)
                poolA.guard[k] = s_pe.n
            gate_pe = s_pe.n
            p.wait("act", s_pe, gate_pe)
            if gi > 0:
                p.wait("act", s_dv, prod_dv)
            gs_sl = {}
            for j in range(ni):
                for tt in range(NTT):
                    base = j * NTT + tt
                    slc = scr(base // 2, 1024)[:, (base % 2) * TT:(base % 2) * TT + TT]
                    gs_sl[(j, tt)] = slc
                    p.op("act", lambda e, o=slc, i=gbanks[j * 2 + tt][:]:
                         e.activation(out=o, in_=i, func=AF.Silu), inc=s_ac, writes=("gs",))
            silu_ac = s_ac.n
            p.wait("pe", s_dv, prod_dv)
            for k in range(NK):
                slu = poolB.slot(k)
                p.wait("pe", ldu[k][0], ldu[k][1])
                for j in range(ni):
                    for tt in range(NTT):
                        p.op("pe", lambda e, o=ubanks[j * 2 + tt][:], w=slu[:, j * 128:(j + 1) * 128],
                             r=xnT_k(k, tt), st=(k == 0), fin=(k == NK - 1):
                             e.matmul(out=o, lhsT=w, rhs=r, start=st, stop=fin))
                p.milestone("pe", s_pe)
                poolB.guard[k] = s_pe.n
            up_pe = s_pe.n
            p.wait("dve", s_pe, up_pe)
            p.wait("dve", s_ac, silu_ac)
            for j in range(ni):
                for tt in range(NTT):
                    tmp = scr(3, 1024)[:, tt * TT:tt * TT + TT]
                    p.dve(lambda e, o=tmp, a=ubanks[j * 2 + tt][:], bb=CBC[:, tt * TT:(tt + 1) * TT]:
                          e.tensor_mul(out=o, in0=a, in1=bb), reads=("cbc",), writes=("tmp",))
                    p.dve(lambda e, o=hp_k(par, i0 + j, tt), a=gs_sl[(j, tt)], bb=tmp:
                          e.tensor_mul(out=o, in0=a, in1=bb), reads=("tmp", "gs"), writes=("hp",))
            p.milestone("dve", s_dv)
            prod_dv = s_dv.n
        hp_dv = s_dv.n
        acc_dv = {}
        for ocp in range(8):
            banks = [ps[(ocp % 2) * 4 + i] for i in range(4)]
            wd_ms = []

            def issue(i, ex=ex, ocp=ocp, wd_ms=wd_ms):
                wd_ms.append(pool_load(
                    poolC, i, w_down[ex, i * 128:(i + 1) * 128, ocp * 256:(ocp + 1) * 256]))

            for i in range(4):
                issue(i)
            for i in range(NI):
                p.wait("pe", wd_ms[i][0], wd_ms[i][1])
                if i == 0:
                    p.wait("pe", s_dv, hp_dv)
                    if ocp == 0:
                        p.wait("pe", s_ac)
                    elif ocp >= 2:
                        p.wait("pe", s_dv, acc_dv[ocp - 2])
                sl = poolC.slot(i)
                for j in range(2):
                    for tt in range(NTT):
                        p.op("pe", lambda e, o=banks[j * 2 + tt][:], w=sl[:, j * 128:(j + 1) * 128],
                             r=hp_k(par, i, tt), st=(i == 0), fin=(i == NI - 1):
                             e.matmul(out=o, lhsT=w, rhs=r, start=st, stop=fin))
                p.milestone("pe", s_pe)
                poolC.guard[i % 8] = s_pe.n
                if i + 4 < NI:
                    issue(i + 4)
            p.wait("dve", s_pe)
            for j in range(2):
                oc = ocp * 2 + j
                for tt in range(NTT):
                    p.dve(lambda e, o=ACC_k(oc, tt), a=ACC_k(oc, tt), bb=banks[j * 2 + tt][:]:
                          e.tensor_add(out=o, in0=a, in1=bb), reads=("acc",), writes=("acc",))
            p.milestone("dve", s_dv)
            acc_dv[ocp] = s_dv.n
    acc_done_dv = s_dv.n
    tap("acc_final", XH[:], (128, NK * 1024), F16)

    p.wait("pe", s_dv, acc_done_dv)
    p.wait("sp", s_dv, acc_done_dv)
    st_done = []
    for tj in range(8):
        xsl = scr(4 * (tj % 2), 2048)
        osl = scr(4 * (tj % 2) + 2, 2048)
        if tj >= 2:
            p.wait("sp", s_st)
        sdma(xsl, x_hbm[tj * 128:(tj + 1) * 128, :], sem=s_x2[tj % 2])
        xld = s_x2[tj % 2].n
        if tj >= 2:
            p.wait("pe", s_dv)
        for q in range(4):
            bank = ps[q]
            for j in range(4):
                k = q * 4 + j
                p.op("pe", lambda e, o=bank[:, j * 128:(j + 1) * 128],
                     w=XH[:, k * 1024 + tj * 128:k * 1024 + (tj + 1) * 128]:
                     e.matmul(out=o, lhsT=w, rhs=ident16[:], start=True, stop=True))
            p.milestone("pe", s_pe)
            p.wait("dve", s_pe)
            p.wait("dve", s_x2[tj % 2], xld)
            if tj >= 2:
                p.wait("dve", s_st)
            p.dve(lambda e, o=osl[:, q * 512:(q + 1) * 512], a=bank[:, 0:512],
                  bb=xsl[:, q * 512:(q + 1) * 512]:
                  e.tensor_add(out=o, in0=a, in1=bb), reads=("xre",), writes=("outtm",))
        p.milestone("dve", s_dv)
        p.wait("pe", s_dv)
        p.wait("sp", s_dv)
        p.op("sp", lambda e, o=out_h[tj * 128:(tj + 1) * 128, :], i=osl: e.dma_start(out=o, in_=i),
             inc=s_st, inck=16)
        st_done.append(s_st.n)
    p.wait("sp", s_st)

    with es:
        def replay(e, lst):
            for ent in lst:
                ins = ent["fn"](e)
                for (h, k) in (ent["incs"] or ()):
                    ins.then_inc(h, k)

        @block.sync
        def _(e):
            replay(e, p.q["sp"])

        @block.tensor
        def _(e):
            replay(e, p.q["pe"])

        @block.vector
        def _(e):
            replay(e, p.q["dve"])

        @block.scalar
        def _(e):
            replay(e, p.q["act"])

        @block.gpsimd
        def _(e):
            replay(e, p.q["gp"])

    return nc


_IDENT = np.eye(128, dtype=np.float32)
_E8SEL = np.zeros((8, 1024), dtype=np.float32)
for _e in range(8):
    _E8SEL[_e, _e * 128:(_e + 1) * 128] = 1.0
_ONES = np.ones((1, 128), dtype=np.float32)


def kernel(L0_attn_x, L0_attn_a_real_param, L0_attn_a_imag_param, L0_attn_W_in,
           L0_attn_b_in, L0_attn_Wg_x, L0_attn_bg_x, L0_attn_Wg_a, L0_attn_bg_a,
           L0_attn_W_out, L0_attn_b_out, L0_ffn_router_w, L0_ffn_w_gate,
           L0_ffn_w_up, L0_ffn_w_down, L0_ffn_sw_gate, L0_ffn_sw_up,
           L0_ffn_sw_down, L0_ffn_shared_gate_w, _taps=(), _nc=None):
    nc = _nc if _nc is not None else build_program(taps=_taps)
    xf = np.ascontiguousarray(np.asarray(L0_attn_x, dtype=np.float32).reshape(NCORES * T, D))
    common = dict(
        W_in=np.ascontiguousarray(L0_attn_W_in, dtype=np.float32),
        b_in=np.ascontiguousarray(L0_attn_b_in, dtype=np.float32).reshape(1, D),
        Wg_x=np.ascontiguousarray(L0_attn_Wg_x, dtype=np.float32),
        bg_x=np.ascontiguousarray(L0_attn_bg_x, dtype=np.float32).reshape(1, DH),
        Wg_a=np.ascontiguousarray(L0_attn_Wg_a, dtype=np.float32),
        bg_a=np.ascontiguousarray(L0_attn_bg_a, dtype=np.float32).reshape(1, DH),
        W_out=np.ascontiguousarray(L0_attn_W_out, dtype=np.float32),
        b_out=np.ascontiguousarray(L0_attn_b_out, dtype=np.float32).reshape(1, D),
        a_real=np.ascontiguousarray(L0_attn_a_real_param, dtype=np.float32).reshape(1, DH),
        a_imag=np.ascontiguousarray(L0_attn_a_imag_param, dtype=np.float32).reshape(1, DH),
        router_w=np.ascontiguousarray(L0_ffn_router_w, dtype=np.float32),
        w_gate=np.ascontiguousarray(L0_ffn_w_gate, dtype=np.float32),
        w_up=np.ascontiguousarray(L0_ffn_w_up, dtype=np.float32),
        w_down=np.ascontiguousarray(L0_ffn_w_down, dtype=np.float32),
        sw_gate=np.ascontiguousarray(L0_ffn_sw_gate, dtype=np.float32),
        sw_up=np.ascontiguousarray(L0_ffn_sw_up, dtype=np.float32),
        sw_down=np.ascontiguousarray(L0_ffn_sw_down, dtype=np.float32),
        sgw=np.ascontiguousarray(L0_ffn_shared_gate_w, dtype=np.float32).reshape(1, D),
        ident=_IDENT,
        e8sel=_E8SEL,
    )
    in_maps = []
    for c in range(NCORES):
        oh = np.zeros((1, 8), dtype=np.float32)
        if c > 0:
            oh[0, c - 1] = 1.0
        m = dict(common)
        m["x"] = np.ascontiguousarray(xf[c * T:(c + 1) * T])
        m["oh8"] = oh
        in_maps.append(m)
    res = run_bass_kernel_spmd(nc, in_maps, core_ids=list(range(NCORES)))
    out = np.concatenate([r["out"] for r in res.results], axis=0)
    main = out.reshape(1, NCORES * T, D).astype(np.float32)
    if _taps:
        return main, res
    return main
